# version 20
# baseline (speedup 1.0000x reference)
import contextlib
import numpy as np
import concourse.bass as bass
import concourse.mybir as mybir
from concourse.bass_utils import run_bass_kernel_spmd

F32 = mybir.dt.float32
BF16 = mybir.dt.bfloat16
AF = mybir.ActivationFunctionType
ALU = mybir.AluOpType
AX = mybir.AxisListType

T = 2048
HF = 1024
D = 1024
CH = 8
NH = 16
LN_EPS = 1e-5
QSCALE = 0.125
NEGBIG = -1e30
STOP_AFTER = 4


def _slopes16():
    start = 2.0 ** (-(2.0 ** -1.0))
    return [start * start ** i for i in range(16)]


QUEUES = ("pe", "dv", "ac", "gp", "sq")


class Op:
    __slots__ = ("q", "fn", "deps", "dma", "inc", "tick", "sem", "chan", "bulk")

    def __init__(self, q, fn, deps, dma, chan=None, bulk=False):
        self.q = q
        self.fn = fn
        self.deps = [d for d in deps if d is not None]
        self.dma = dma
        self.inc = dma
        self.tick = None
        self.sem = None
        self.chan = chan
        self.bulk = bulk


class Sched:
    def __init__(self):
        self.seq = {q: [] for q in QUEUES}
        self.final = []
        self._ins = None

    def _op(self, q, fn, deps, dma=False, chan=None, bulk=False):
        o = Op(q, fn, deps, dma, chan, bulk)
        for d in o.deps:
            d.inc = True
        if self._ins is not None:
            self.seq[q].insert(self._ins[q], o)
            self._ins[q] += 1
        else:
            self.seq[q].append(o)
        return o

    def mark(self):
        return {q: len(self.seq[q]) for q in QUEUES}

    @contextlib.contextmanager
    def at_marks(self, marks):
        prev = self._ins
        self._ins = marks
        try:
            yield
        finally:
            self._ins = prev

    def pe(self, fn, deps=()):
        return self._op("pe", fn, deps)

    def dv(self, fn, deps=()):
        return self._op("dv", fn, deps)

    def ac(self, fn, deps=()):
        return self._op("ac", fn, deps)

    def gp(self, fn, deps=()):
        return self._op("gp", fn, deps)

    def dma_sq(self, fn, chan, deps=(), bulk=False):
        return self._op("sq", fn, deps, dma=True, chan=chan, bulk=bulk)

    def dma_gp(self, fn, chan, deps=(), bulk=False):
        return self._op("gp", fn, deps, dma=True, chan=chan, bulk=bulk)

    def assign(self, sems, mk_dma_sem):
        counters = {k: 0 for k in sems}
        dma_sems = {}
        dma_counters = {}
        bulk_ops = {}
        for q in QUEUES:
            for o in self.seq[q]:
                if not o.inc:
                    continue
                if o.dma:
                    if o.chan not in dma_sems:
                        dma_sems[o.chan] = mk_dma_sem(o.chan)
                        dma_counters[o.chan] = 0
                    dma_counters[o.chan] += 16
                    o.tick = dma_counters[o.chan]
                    o.sem = dma_sems[o.chan]
                    if o.bulk:
                        bulk_ops.setdefault(o.chan, []).append(o)
                else:
                    counters[o.q] += 1
                    o.tick = counters[o.q]
                    o.sem = sems[o.q]
        for chan, ops in bulk_ops.items():
            for o in ops:
                o.tick = dma_counters[chan]

    def emit(self, q, eng):
        hw = {}
        for o in self.seq[q]:
            for d in o.deps:
                key = id(d.sem)
                if hw.get(key, -1) < d.tick:
                    eng.wait_ge(d.sem, d.tick)
                    hw[key] = d.tick
            inst = o.fn(eng)
            if o.inc:
                inst.then_inc(o.sem, 16 if o.dma else 1)
        for (fq, o) in self.final:
            if fq == q:
                eng.wait_ge(o.sem, o.tick)


def build(stop_after=4, use_recip=False, dm_gp=True):
    nc = bass.Bass("TRN2", target_bir_lowering=False)
    s = Sched()
    slopes = _slopes16()
    _ncnt = [0]

    def _nm(b):
        _ncnt[0] += 1
        return f"{b}{_ncnt[0]}"

    d_x = nc.dram_tensor("xT", [D, T], F32, kind="ExternalInput").ap()
    dw = {}
    for nm in ("wqT", "wvT", "wgT", "woT", "wrT", "wkT", "wfvT", "wout"):
        dw[nm] = nc.dram_tensor(nm, [D, D], F32, kind="ExternalInput").ap()
    d_wqkv = nc.dram_tensor("wqkv", [D, 3 * D], F32, kind="ExternalInput").ap()
    d_bout = nc.dram_tensor("bout_c", [128, CH], F32, kind="ExternalInput").ap()
    d_mur = nc.dram_tensor("mur_c", [128, CH], F32, kind="ExternalInput").ap()
    d_muk = nc.dram_tensor("muk_c", [128, CH], F32, kind="ExternalInput").ap()
    d_gam = nc.dram_tensor("gam_c", [128, CH], F32, kind="ExternalInput").ap()
    d_bet = nc.dram_tensor("bet_c", [128, CH], F32, kind="ExternalInput").ap()
    d_betapc = nc.dram_tensor("beta_pc", [128, CH], F32, kind="ExternalInput").ap()
    d_rel = nc.dram_tensor("relmaster", [128, T], F32, kind="ExternalInput").ap()
    d_Dm = nc.dram_tensor("Dmaster", [NH * 128, T], BF16, kind="ExternalInput").ap()
    d_G = nc.dram_tensor("Gind", [128, 32], BF16, kind="ExternalInput").ap()
    d_idn = nc.dram_tensor("idn", [128, 128], BF16, kind="ExternalInput").ap()
    d_out = nc.dram_tensor("out", [D, T], F32, kind="ExternalOutput").ap()
    d_xs1 = nc.dram_tensor("xscr1", [D, T], F32, kind="Internal").ap()
    d_xs2 = nc.dram_tensor("xscr2", [D, T], F32, kind="Internal").ap()

    es = contextlib.ExitStack()
    with es:
        sems = {}
        for k in ("pe", "dv", "ac", "gp"):
            sems[k] = es.enter_context(nc.semaphore(k))

        def mk_dma_sem(chan):
            return es.enter_context(nc.semaphore(f"dma_{chan}"))

        xn = es.enter_context(nc.sbuf_tensor("xn", [128, CH * T], BF16))
        arena = es.enter_context(nc.sbuf_tensor("arena", [128, CH * T], F32))
        arena_bf = arena.ap().bitcast(BF16)
        sq_scr = es.enter_context(nc.sbuf_tensor("sq_scr", [128, T], BF16))
        w_stg = es.enter_context(nc.sbuf_tensor("w_stg", [128, 1024], F32))
        w_cache = es.enter_context(nc.sbuf_tensor("w_cache", [128, 4096], BF16))
        ln_scr = es.enter_context(nc.sbuf_tensor("ln_scr", [128, 2048], F32))
        p_scr = es.enter_context(nc.sbuf_tensor("p_scr", [128, 4096], F32))
        o128 = es.enter_context(nc.sbuf_tensor("o128", [128, 128], BF16))
        ones_rb = es.enter_context(nc.sbuf_tensor("ones_rb", [1, 128], BF16))
        lam_c = es.enter_context(nc.sbuf_tensor("lam_c", [128, CH], F32))
        betap = es.enter_context(nc.sbuf_tensor("betap", [128, CH], F32))
        murc = es.enter_context(nc.sbuf_tensor("murc", [128, CH], F32))
        mukc = es.enter_context(nc.sbuf_tensor("mukc", [128, CH], F32))
        gamc = es.enter_context(nc.sbuf_tensor("gamc", [128, CH], F32))
        betc = es.enter_context(nc.sbuf_tensor("betc", [128, CH], F32))
        boutc = es.enter_context(nc.sbuf_tensor("boutc", [128, CH], F32))
        Gind = es.enter_context(nc.sbuf_tensor("Gind_sb", [128, 32], BF16))
        idn = es.enter_context(nc.sbuf_tensor("idn_sb", [128, 128], BF16))

        def xnv(cc):
            return xn[:, cc * T:(cc + 1) * T]

        def arv(cc):
            return arena[:, cc * T:(cc + 1) * T]

        c_o128 = s.dv(lambda e: e.memset(o128[:], 1.0))
        c_rb = s.dv(lambda e: e.memset(ones_rb[:], 1.0))
        ld = {}
        for key, dst, src in (("betap", betap, d_betapc), ("murc", murc, d_mur),
                              ("mukc", mukc, d_muk), ("gamc", gamc, d_gam),
                              ("betc", betc, d_bet), ("boutc", boutc, d_bout),
                              ("Gind", Gind, d_G), ("idn", idn, d_idn)):
            ld[key] = s.dma_gp(lambda e, dst=dst, src=src: e.dma_start(out=dst[:], in_=src),
                               "const", bulk=True)
        lam_op = s.ac(lambda e: e.activation(lam_c[:], betap[:], AF.Sigmoid), deps=[ld["betap"]])

        g_stg = {0: [], 1: [], 2: [], 3: []}
        g_cache = {0: [], 1: []}
        gate = {"ps": []}
        g_chan = {}

        def mk_dram_src(dram_ap, buf):
            state = {"cnt": 0}

            def src(cc, half):
                slot = state["cnt"] % 4
                state["cnt"] += 1
                chan = f"xa{slot}"
                emit = s.dma_sq if slot < 2 else s.dma_gp
                d = emit(lambda e, cc=cc, half=half, slot=slot: e.dma_start(
                    out=buf[:, slot * HF:slot * HF + HF],
                    in_=dram_ap[cc * 128:(cc + 1) * 128, half * HF:half * HF + HF]),
                    chan, deps=g_chan.get(chan, []))
                marks = []
                g_chan[chan] = marks
                return buf[:, slot * HF:slot * HF + HF], [d], marks
            return src

        def mk_sbuf_src(view_fn, deps_fn):
            def src(cc, half):
                return view_fn(cc)[:, half * HF:half * HF + HF], list(deps_fn(cc)), []
            return src

        def emit_ln(src_fn, bcs, bch, cast_dst_fn, apply_dst_fn=None,
                    apply_from_cast=False, extra_src_deps_fn=None,
                    apply_gp_cut=6):
            apply_ops = [[] for _ in range(CH)]
            apply_tt = {}
            bc_done = []
            csts = {}
            with contextlib.ExitStack() as les:
                ps_sum = les.enter_context(nc.psum_tensor(_nm("ps_su"), [128, HF], F32))
                ps_sq = les.enter_context(nc.psum_tensor(_nm("ps_sq"), [128, HF], F32))
                stats_war = list(gate["ps"])
                sq_war = {0: [], 1: []}
                apply_war = {i: [] for i in range(4)}
                for half in range(2):
                    h0 = half * HF
                    mm_sum_last = []
                    mm_sq_last = []
                    for cc in range(CH):
                        src, sdeps, marks = src_fn(cc, half)
                        if extra_src_deps_fn is not None:
                            sdeps = sdeps + extra_src_deps_fn(half)
                        cdst = cast_dst_fn(cc)
                        so = (cc % 2) * HF
                        if cc % 2 == 0:
                            cst = s.ac(lambda e, src=src, cdst=cdst, h0=h0: e.activation(
                                cdst[:, h0:h0 + HF], src, AF.Identity), deps=sdeps)
                            sqc = s.dv(lambda e, src=src, so=so: e.tensor_tensor(
                                sq_scr[:, so:so + HF], src, src, ALU.mult),
                                deps=sdeps + sq_war[cc % 2])
                        else:
                            cst = s.dv(lambda e, src=src, cdst=cdst, h0=h0: e.tensor_copy(
                                cdst[:, h0:h0 + HF], src), deps=sdeps)
                            sqc = s.ac(lambda e, src=src, so=so: e.activation(
                                sq_scr[:, so:so + HF], src, AF.Square),
                                deps=sdeps + sq_war[cc % 2])
                        marks += [cst, sqc]
                        csts[(cc, half)] = cst
                        mm_q = []
                        mm_s = []
                        for ip in range(2):
                            mm_s.append(s.pe(lambda e, cdst=cdst, h0=h0, ip=ip, cc=cc: e.matmul(
                                ps_sum[:, ip * 512:(ip + 1) * 512], o128[:],
                                cdst[:, h0 + ip * 512:h0 + (ip + 1) * 512],
                                start=(cc == 0), stop=(cc == CH - 1)),
                                deps=[cst, c_o128] + (stats_war if cc == 0 else [])))
                            mm_q.append(s.pe(lambda e, so=so, ip=ip, cc=cc: e.matmul(
                                ps_sq[:, ip * 512:(ip + 1) * 512], o128[:],
                                sq_scr[:, so + ip * 512:so + (ip + 1) * 512],
                                start=(cc == 0), stop=(cc == CH - 1)),
                                deps=[sqc, c_o128] + (stats_war if cc == 0 else [])))
                        sq_war[cc % 2] = mm_q
                        mm_sum_last = mm_s
                        mm_sq_last = mm_q
                    rs = bcs[:, h0:h0 + HF]
                    rh = bch[:, h0:h0 + HF]
                    nm_v = ln_scr[:, h0:h0 + HF]
                    o1 = s.dv(lambda e, nm_v=nm_v: e.tensor_scalar_mul(
                        nm_v, ps_sum[:, 0:HF], -1.0 / D), deps=mm_sum_last)
                    o2 = s.dv(lambda e, rh=rh: e.tensor_scalar(
                        rh, ps_sq[:, 0:HF], 1.0 / D, LN_EPS, ALU.mult, ALU.add),
                        deps=mm_sq_last)
                    o3 = s.dv(lambda e, nm_v=nm_v: e.tensor_tensor(
                        ps_sum[:, 0:HF], nm_v, nm_v, ALU.mult), deps=[o1])
                    o4 = s.dv(lambda e, rh=rh: e.tensor_tensor(
                        rh, rh, ps_sum[:, 0:HF], ALU.subtract), deps=[o2, o3])
                    o5 = s.ac(lambda e, rh=rh: e.activation(
                        ps_sq[:, 0:HF], rh, AF.Ln), deps=[o4])
                    o6 = s.ac(lambda e, rs=rs: e.activation(
                        rs, ps_sq[:, 0:HF], AF.Exp, scale=-0.5), deps=[o5])
                    o7 = s.dv(lambda e, rh=rh, nm_v=nm_v, rs=rs: e.tensor_tensor(
                        rh, nm_v, rs, ALU.mult), deps=[o6, o4])
                    stats_war = [o4, o6]
                    bc_done += [o6, o7]
                    if apply_dst_fn is not None:
                        for cc in range(CH):
                            if apply_from_cast:
                                aview0 = cast_dst_fn(cc)[:, h0:h0 + HF]
                                adeps = [csts[(cc, half)]]
                            else:
                                aview0, adeps, _m = src_fn(cc, half)
                            for q in range(2):
                                q0 = q * 512
                                aview = aview0[:, q0:q0 + 512]
                                eng = s.dv if cc < apply_gp_cut else s.gp
                                slot = (cc * 2 + q) % 4
                                sB = p_scr[:, slot * 512:slot * 512 + 512]
                                t1 = eng(lambda e, aview=aview, sB=sB, rs=rs, q0=q0:
                                         e.tensor_tensor(
                                    sB, aview, rs[:, q0:q0 + 512], ALU.mult),
                                    deps=adeps + [o6] + apply_war[slot])
                                t2 = eng(lambda e, cc=cc, sB=sB, rh=rh, h0=h0, q0=q0:
                                         e.tensor_tensor(
                                    apply_dst_fn(cc)[:, h0 + q0:h0 + q0 + 512], sB,
                                    rh[:, q0:q0 + 512], ALU.add),
                                    deps=[t1, o7])
                                apply_war[slot] = [t2]
                                apply_ops[cc].append(t2)
                                apply_tt[(cc, half * 2 + q)] = t2
                gate["ps"] = list(stats_war)
            return (apply_ops if apply_dst_fn is not None else None), apply_tt, bc_done

        def proj_fm(wdram, col0, rhs_fn, rhs_deps_fn, evac_fn, ps_proj, war_fn=None,
                    second=None, nreg=2, pre_marks=None, pre_dg2=0):
            out_ops = {}
            cache_users = g_cache
            region_users = {}
            piece = 0
            stg_cnt = 0
            nW = 2 if second is not None else 1
            for dg2 in range(4):
                use_mark = pre_marks is not None and dg2 < pre_dg2
                for wi in range(nW):
                    wd, c0 = (wdram, col0) if wi == 0 else (second[0], second[1])
                    key = (dg2 % 2) if nW == 1 else wi
                    consumed = cache_users[key]
                    cache_users[key] = []
                    for c in range(CH):
                        slot = stg_cnt % 4
                        stg_cnt += 1
                        stv = w_stg[:, slot * 256:slot * 256 + 256]
                        dma = s.dma_sq(lambda e, c=c, dg2=dg2, stv=stv, wd=wd, c0=c0: e.dma_start(
                            out=stv, in_=wd[c * 128:(c + 1) * 128,
                                            c0 + dg2 * 256:c0 + dg2 * 256 + 256]),
                            f"wstg{slot}", deps=g_stg[slot])
                        if nW == 1:
                            dstv = w_cache[:, (dg2 % 2) * 2048 + c * 256:(dg2 % 2) * 2048 + c * 256 + 256]
                        else:
                            dstv = w_cache[:, wi * 2048 + c * 256:wi * 2048 + c * 256 + 256]
                        ctx = s.at_marks(pre_marks) if use_mark else contextlib.nullcontext()
                        with ctx:
                            if (c + dg2) % 2 == 0:
                                cst = s.dv(lambda e, stv=stv, dstv=dstv: e.tensor_copy(dstv, stv),
                                           deps=[dma] + consumed)
                            else:
                                cst = s.ac(lambda e, stv=stv, dstv=dstv: e.activation(
                                    dstv, stv, AF.Identity), deps=[dma] + consumed)
                        g_stg[slot] = [cst]
                        out_ops.setdefault(("cast", (dg2, wi)), {})[c] = cst
                for tt in range(4):
                    par = piece % nreg
                    for wi in range(nW):
                        rfn = rhs_fn if wi == 0 else second[2]
                        rdeps = rhs_deps_fn if wi == 0 else second[3]
                        efn = evac_fn if wi == 0 else second[4]
                        psp = ps_proj if wi == 0 else second[5]
                        mms_dd = {0: [], 1: []}
                        for ci in range(CH):
                            for dd in range(2):
                                pp = psp[:, par * 1024 + dd * 512: par * 1024 + dd * 512 + 512]
                                deps = [out_ops[("cast", (dg2, wi))][ci]] + rdeps(ci, tt)
                                if ci == 0:
                                    deps = deps + region_users.get((wi, par), gate["ps"])
                                    if war_fn is not None and wi == 0:
                                        deps = deps + war_fn(dg2 * 2 + dd, tt)
                                if nW == 1:
                                    wv = w_cache[:, (dg2 % 2) * 2048 + ci * 256 + dd * 128:
                                                 (dg2 % 2) * 2048 + ci * 256 + dd * 128 + 128]
                                else:
                                    wv = w_cache[:, wi * 2048 + ci * 256 + dd * 128:
                                                 wi * 2048 + ci * 256 + dd * 128 + 128]
                                mm = s.pe(lambda e, wv=wv, pp=pp, ci=ci, tt=tt, rfn=rfn: e.matmul(
                                    pp, wv, rfn(ci)[:, tt * 512:(tt + 1) * 512],
                                    start=(ci == 0), stop=(ci == CH - 1)), deps=deps)
                                mms_dd[dd].append(mm)
                                cache_users[(dg2 % 2) if nW == 1 else wi].append(mm)
                        evs_all = []
                        for dd in range(2):
                            oc = dg2 * 2 + dd
                            pp = psp[:, par * 1024 + dd * 512: par * 1024 + dd * 512 + 512]
                            evs = efn(oc, tt, pp, mms_dd[dd][-1], par)
                            out_ops[(wi, oc, tt)] = evs
                            evs_all += evs
                        region_users[(wi, par)] = evs_all
                    piece += 1
            gate["ps"] = [op for v in region_users.values() for op in v]
            return out_ops

        a_es = contextlib.ExitStack()
        bcs_a = a_es.enter_context(nc.sbuf_tensor(_nm("bcs_a"), [128, T], F32))
        bch_a = a_es.enter_context(nc.sbuf_tensor(_nm("bch_a"), [128, T], F32))
        xsc_a = a_es.enter_context(nc.sbuf_tensor(_nm("xsc_a"), [128, 4 * HF], F32))
        src_a = mk_dram_src(d_x, xsc_a)
        applyA, applyA_tt, _ = emit_ln(src_a, bcs_a, bch_a, cast_dst_fn=xnv,
                                       apply_dst_fn=xnv, apply_from_cast=True,
                                       apply_gp_cut=5)
        a_es.close()

        v_ops = {}
        with contextlib.ExitStack() as les:
            ps_proj = les.enter_context(nc.psum_tensor(_nm("ps_va"), [128, 4096], F32))

            def ev_v(oc, tt, pp, mml, par):
                ev = s.ac(lambda e, oc=oc, tt=tt, pp=pp: e.activation(
                    arv(oc)[:, tt * 512:(tt + 1) * 512], pp, AF.Identity), deps=[mml])
                v_ops[(oc, tt)] = ev
                return [ev]

            proj_fm(dw["wvT"], 0, xnv, lambda ci, tt: [applyA_tt[(ci, tt)]], ev_v, ps_proj, nreg=4)

        marks_a = s.mark()

        st_es = contextlib.ExitStack()
        states = st_es.enter_context(nc.sbuf_tensor(_nm("states"), [128, CH * T], F32))

        def stv(cc):
            return states[:, cc * T:(cc + 1) * T]

        scan_ops = []
        for cc in range(CH):
            eng = s.dv
            sc = eng(lambda e, cc=cc: e.tensor_tensor_scan(
                stv(cc), lam_c[:, cc:cc + 1].broadcast_to([128, T]), arv(cc), 0.0,
                ALU.mult, ALU.add),
                deps=[v_ops[(cc, tt)] for tt in range(4)] + [lam_op])
            scan_ops.append(sc)

        y_ops = {}
        with contextlib.ExitStack() as les:
            ps_proj = les.enter_context(nc.psum_tensor(_nm("ps_qa"), [128, 4096], F32))

            def ev_q(oc, tt, pp, mml, par):
                ev = s.dv(lambda e, oc=oc, tt=tt, pp=pp: e.tensor_tensor(
                    arv(oc)[:, tt * 512:(tt + 1) * 512], pp,
                    stv(oc)[:, tt * 512:(tt + 1) * 512], ALU.mult),
                    deps=[mml, scan_ops[oc]])
                y_ops[(oc, tt)] = ev
                return [ev]

            proj_fm(dw["wqT"], 0, xnv, lambda ci, tt: [applyA_tt[(ci, tt)]], ev_q, ps_proj,
                    nreg=4, pre_marks=marks_a, pre_dg2=2)

        marks_g = s.mark()

        st_es.close()
        yg_es = contextlib.ExitStack()
        bcs_y = yg_es.enter_context(nc.sbuf_tensor(_nm("bcs_y"), [128, T], F32))
        bch_y = yg_es.enter_context(nc.sbuf_tensor(_nm("bch_y"), [128, T], F32))
        y_g = yg_es.enter_context(nc.sbuf_tensor(_nm("y_g"), [128, CH * T], BF16))

        def ygv(cc):
            return y_g[:, cc * T:(cc + 1) * T]

        src_y = mk_sbuf_src(arv, lambda cc: [y_ops[(cc, tt)] for tt in range(4)])
        all_yops = [y_ops[(c, t)] for c in range(CH) for t in range(4)]
        _, _, bc_y = emit_ln(src_y, bcs_y, bch_y, cast_dst_fn=ygv,
                             extra_src_deps_fn=lambda half: all_yops)

        yg_ops = {}
        y_free = {}
        with contextlib.ExitStack() as les:
            ps_proj = les.enter_context(nc.psum_tensor(_nm("ps_ga"), [128, 4096], F32))

            def ev_g(oc, tt, pp, mml, par):
                sl = slice(tt * 512, (tt + 1) * 512)
                sB = p_scr[:, (par * 2) * 512:(par * 2) * 512 + 512]
                sC = p_scr[:, (par * 2 + 1) * 512:(par * 2 + 1) * 512 + 512]
                t1 = s.dv(lambda e, oc=oc, sl=sl, sB=sB: e.tensor_tensor(
                    sB, arv(oc)[:, sl], bcs_y[:, sl], ALU.mult),
                    deps=[y_ops[(oc, tt)]] + bc_y + y_free.get((par, 0), []))
                t2 = s.dv(lambda e, sl=sl, sB=sB, sC=sC: e.tensor_tensor(
                    sC, sB, bch_y[:, sl], ALU.add), deps=[t1] + y_free.get((par, 1), []))
                yn = s.ac(lambda e, oc=oc, sB=sB, sC=sC: e.activation(
                    sB, sC, AF.Identity, bias=betc[:, oc:oc + 1], scale=gamc[:, oc:oc + 1]),
                    deps=[t2, ld["gamc"], ld["betc"]])
                gate = s.ac(lambda e, pp=pp, sC=sC: e.activation(sC, pp, AF.Silu),
                            deps=[mml, yn])
                yg = s.gp(lambda e, oc=oc, sl=sl, sB=sB, sC=sC: e.tensor_tensor(
                    ygv(oc)[:, sl], sB, sC, ALU.mult), deps=[gate, yn])
                yg_ops[(oc, tt)] = yg
                y_free[(par, 0)] = [yn, yg]
                y_free[(par, 1)] = [gate, yg]
                return [gate]

            proj_fm(dw["wgT"], 0, xnv, lambda ci, tt: [applyA_tt[(ci, tt)]], ev_g, ps_proj,
                    nreg=4, pre_marks=marks_g, pre_dg2=2)

        res1_ops = {}
        with contextlib.ExitStack() as les:
            ps_proj = les.enter_context(nc.psum_tensor(_nm("ps_wa"), [128, 4096], F32))
            xsc2 = les.enter_context(nc.sbuf_tensor(_nm("xsc2_a"), [128, 2 * T], F32))
            xdma = {}
            states_dead = [y_ops[(c, t)] for c in range(CH) for t in range(4)]

            def ev_wo(oc, tt, pp, mml, par):
                slot = oc % 2
                chan = f"str{slot}"
                if oc not in xdma:
                    xdma[oc] = s.dma_gp(lambda e, oc=oc, slot=slot: e.dma_start(
                        out=xsc2[:, slot * T:slot * T + T],
                        in_=d_x[oc * 128:(oc + 1) * 128, :]), chan,
                        deps=g_chan.get(chan, []) + states_dead)
                    g_chan[chan] = []
                ev = s.dv(lambda e, oc=oc, tt=tt, pp=pp, slot=slot: e.tensor_tensor(
                    arv(oc)[:, tt * 512:(tt + 1) * 512],
                    xsc2[:, slot * T + tt * 512: slot * T + (tt + 1) * 512], pp, ALU.add),
                    deps=[mml, xdma[oc], yg_ops[(oc, tt)]])
                res1_ops[(oc, tt)] = ev
                g_chan[chan].append(ev)
                return [ev]

            proj_fm(dw["woT"], 0, ygv, lambda ci, tt: [yg_ops[(ci, tt)]],
                    ev_wo, ps_proj, nreg=4)
        yg_es.close()

        def x1_ready(cc):
            return [res1_ops[(cc, tt)] for tt in range(4)]

        def ffn_layer(resid_in_ops):
            marks_f = s.mark()
            f_es = contextlib.ExitStack()
            with f_es:
                bcs = f_es.enter_context(nc.sbuf_tensor(_nm("bcs_f"), [128, T], F32))
                bch = f_es.enter_context(nc.sbuf_tensor(_nm("bch_f"), [128, T], F32))
                src = mk_sbuf_src(arv, lambda cc: resid_in_ops(cc))
                applyF, applyF_tt, _ = emit_ln(
                    src, bcs, bch, cast_dst_fn=xnv, apply_dst_fn=xnv,
                    extra_src_deps_fn=lambda half: [
                        op for c in range(CH) for op in resid_in_ops(c)])
            m_es = contextlib.ExitStack()
            xr = m_es.enter_context(nc.sbuf_tensor(_nm("xr_f"), [128, CH * T], BF16))
            xk = m_es.enter_context(nc.sbuf_tensor(_nm("xk_f"), [128, CH * T], BF16))
            d_s = m_es.enter_context(nc.sbuf_tensor(_nm("d_f"), [128, 3072], BF16))

            def xrv(cc):
                return xr[:, cc * T:(cc + 1) * T]

            def xkv(cc):
                return xk[:, cc * T:(cc + 1) * T]

            mix_r = {}
            mix_k = {}
            mix_all = {}
            d_war = {i: [] for i in range(4)}
            for g in (1, 2):
                for i in range(4):
                    d_war[(g, i)] = []
            for cc in range(CH):
                for tt in range(4):
                    t0 = tt * 512
                    hi = t0 + 512
                    adep = [applyF_tt[(cc, tt)]]
                    if tt > 0:
                        adep.append(applyF_tt[(cc, tt - 1)])
                    sl = (cc + tt) % 2
                    dsv = d_s[:, sl * 512:sl * 512 + 512]
                    dops = []
                    if tt == 0:
                        dops.append(s.gp(lambda e, cc=cc, dsv=dsv: e.tensor_copy(
                            dsv[:, 0:1], xnv(cc)[:, 0:1]), deps=adep + d_war[sl]))
                        dops.append(s.gp(lambda e, cc=cc, dsv=dsv: e.tensor_tensor(
                            dsv[:, 1:512], xnv(cc)[:, 1:512], xnv(cc)[:, 0:511],
                            ALU.subtract), deps=adep + d_war[sl]))
                    else:
                        dops.append(s.gp(lambda e, cc=cc, dsv=dsv, t0=t0, hi=hi:
                                         e.tensor_tensor(
                            dsv[:], xnv(cc)[:, t0:hi], xnv(cc)[:, t0 - 1:hi - 1],
                            ALU.subtract), deps=adep + d_war[sl]))
                    mrv = d_s[:, 1024 + sl * 512:1024 + sl * 512 + 512]
                    mkv = d_s[:, 2048 + sl * 512:2048 + sl * 512 + 512]
                    mr = s.dv(lambda e, dsv=dsv, mrv=mrv, cc=cc: e.tensor_scalar_mul(
                        mrv, dsv[:], murc[:, cc:cc + 1]),
                        deps=dops + [ld["murc"]] + d_war[(1, sl)])
                    mk = s.dv(lambda e, dsv=dsv, mkv=mkv, cc=cc: e.tensor_scalar_mul(
                        mkv, dsv[:], mukc[:, cc:cc + 1]),
                        deps=dops + [ld["mukc"]] + d_war[(2, sl)])
                    r1 = s.dv(lambda e, cc=cc, mrv=mrv, t0=t0, hi=hi: e.tensor_tensor(
                        xrv(cc)[:, t0:hi], mrv, xnv(cc)[:, t0:hi], ALU.add),
                        deps=[mr])
                    k1 = s.dv(lambda e, cc=cc, mkv=mkv, t0=t0, hi=hi: e.tensor_tensor(
                        xkv(cc)[:, t0:hi], mkv, xnv(cc)[:, t0:hi], ALU.add),
                        deps=[mk])
                    mix_r[(cc, tt)] = [r1]
                    mix_k[(cc, tt)] = [k1]
                    mix_all[(cc, tt)] = dops + [r1, k1]
                    d_war[sl] = [mr, mk]
                    d_war[(1, sl)] = [r1]
                    d_war[(2, sl)] = [k1]

            kact_ops = {}
            k_scratch = []
            with contextlib.ExitStack() as les:
                ps_proj = les.enter_context(nc.psum_tensor(_nm("ps_k"), [128, 4096], F32))

                def ev_k(oc, tt, pp, mml, par):
                    sB = p_scr[:, par * 1024:par * 1024 + 512]
                    t1 = s.dv(lambda e, pp=pp, sB=sB: e.tensor_scalar_max(sB, pp, 0.0),
                              deps=[mml] + kact_ops.get(("free", par), []))
                    t2 = s.ac(lambda e, oc=oc, tt=tt, sB=sB: e.activation(
                        xnv(oc)[:, tt * 512:(tt + 1) * 512], sB, AF.Square),
                        deps=[t1] + mix_all[(oc, tt)]
                        + (mix_all[(oc, tt + 1)] if tt < 3 else []))
                    kact_ops[(oc, tt)] = t2
                    kact_ops[("free", par)] = [t2]
                    k_scratch.append(t2)
                    return [t1, t2]

                proj_fm(dw["wkT"], 0, xkv, lambda ci, tt: mix_k[(ci, tt)],
                        ev_k, ps_proj, nreg=4, pre_marks=marks_f, pre_dg2=2)

            res_ops = {}
            with contextlib.ExitStack() as les:
                ps_r = les.enter_context(nc.psum_tensor(_nm("ps_r"), [128, 2048], F32))
                ps_v = les.enter_context(nc.psum_tensor(_nm("ps_v2"), [128, 2048], F32))
                t3_scr = p_scr[:, 3072:4096]
                sig_ops = {}

                def ev_r(oc, tt, pp, mml, par):
                    dd = oc % 2
                    sB = p_scr[:, (par * 2 + dd) * 512:(par * 2 + dd) * 512 + 512]
                    sg = s.ac(lambda e, pp=pp, sB=sB: e.activation(sB, pp, AF.Sigmoid),
                              deps=[mml] + sig_ops.get(("free", par, dd), k_scratch))
                    sig_ops[(oc, tt)] = (sg, sB)
                    return [sg]

                def ev_v2(oc, tt, pp, mml, par):
                    dd = oc % 2
                    sg, sB = sig_ops[(oc, tt)]
                    t3 = t3_scr[:, dd * 512:(dd + 1) * 512]
                    m1 = s.dv(lambda e, pp=pp, sB=sB, t3=t3: e.tensor_tensor(
                        t3, sB, pp, ALU.mult), deps=[mml, sg] + res_ops.get(("free", dd), []))
                    sig_ops[("free", par, dd)] = [m1]
                    rr = s.gp(lambda e, oc=oc, tt=tt, t3=t3: e.tensor_tensor(
                        arv(oc)[:, tt * 512:(tt + 1) * 512],
                        arv(oc)[:, tt * 512:(tt + 1) * 512], t3, ALU.add),
                        deps=[m1] + resid_in_ops(oc))
                    res_ops[(oc, tt)] = rr
                    res_ops[("free", dd)] = [rr]
                    return [m1]

                proj_fm(dw["wrT"], 0, xrv, lambda ci, tt: mix_r[(ci, tt)],
                        ev_r, ps_r,
                        second=(dw["wfvT"], 0, xnv,
                                lambda ci, tt: [kact_ops[(ci, tt)]],
                                ev_v2, ps_v))
            m_es.close()
            return lambda cc: [res_ops[(cc, tt)] for tt in range(4)]

        x2_ready = ffn_layer(x1_ready) if stop_after >= 2 else x1_ready

        if stop_after >= 3:
            b_es = contextlib.ExitStack()
            bcs_b = b_es.enter_context(nc.sbuf_tensor(_nm("bcs_b"), [128, T], F32))
            bch_b = b_es.enter_context(nc.sbuf_tensor(_nm("bch_b"), [128, T], F32))
            marks_b = s.mark()
            src_b = mk_sbuf_src(arv, lambda cc: x2_ready(cc))
            all_res_b = [op for c in range(CH) for op in x2_ready(c)]
            applyB, applyB_tt, _ = emit_ln(src_b, bcs_b, bch_b, cast_dst_fn=xnv,
                                           apply_dst_fn=xnv,
                                           extra_src_deps_fn=lambda half: all_res_b,
                                           apply_gp_cut=4)
            b_es.close()
            spill = {}
            for cc in range(CH):
                spill[cc] = s.dma_gp(lambda e, cc=cc: e.dma_start(
                    out=d_xs1[cc * 128:(cc + 1) * 128, :], in_=arv(cc)),
                    f"spill{cc}", deps=x2_ready(cc) + applyB[cc])

            def qv(cc):
                return arena_bf[:, cc * T:(cc + 1) * T]

            def kv(cc):
                return arena_bf[:, (CH + cc) * T:(CH + cc + 1) * T]

            q_ops = {}
            k_ops = {}
            qk_es = contextlib.ExitStack()
            v_tok = qk_es.enter_context(nc.sbuf_tensor(_nm("v_tok"), [128, 16 * 1024], BF16))
            vs_es = contextlib.ExitStack()
            v_scr = vs_es.enter_context(nc.sbuf_tensor(_nm("v_scr"), [128, 2 * T], BF16))
            vt_r = v_tok.ap().rearrange("p (j c) -> p j c", c=1024)
            vtok_ready = {}
            with contextlib.ExitStack() as les:
                ps_proj = les.enter_context(nc.psum_tensor(_nm("ps_q"), [128, 2048], F32))

                def ev_qb(oc, tt, pp, mml, par):
                    ev = s.ac(lambda e, oc=oc, tt=tt, pp=pp: e.activation(
                        qv(oc)[:, tt * 512:(tt + 1) * 512], pp, AF.Identity, scale=QSCALE),
                        deps=[mml, spill[oc // 2]])
                    q_ops[(oc, tt)] = ev
                    return [ev]

                proj_fm(d_wqkv, 0, xnv, lambda ci, tt: [applyB_tt[(ci, tt)]], ev_qb, ps_proj,
                        pre_marks=marks_b, pre_dg2=2)

                def ev_kb(oc, tt, pp, mml, par):
                    ev = s.ac(lambda e, oc=oc, tt=tt, pp=pp: e.activation(
                        kv(oc)[:, tt * 512:(tt + 1) * 512], pp, AF.Identity),
                        deps=[mml, spill[4 + oc // 2]])
                    k_ops[(oc, tt)] = ev
                    return [ev]

                proj_fm(d_wqkv, D, xnv, lambda ci, tt: [applyB_tt[(ci, tt)]], ev_kb, ps_proj)

                ps_tr = les.enter_context(nc.psum_tensor(_nm("ps_tr"), [128, 2048], BF16))
                vscr_ops = {}
                vscr_war = {0: [], 1: []}
                tr_war = {i: list(gate["ps"]) for i in range(8)}

                def ev_vb(oc, tt, pp, mml, par):
                    slot = oc % 2
                    ev = s.ac(lambda e, oc=oc, tt=tt, pp=pp, slot=slot: e.activation(
                        v_scr[:, slot * T + tt * 512: slot * T + (tt + 1) * 512], pp,
                        AF.Identity), deps=[mml] + vscr_war[slot])
                    vscr_ops.setdefault(oc, []).append(ev)
                    ret = [ev]
                    if tt == 3:
                        war_new = []
                        for g in range(4):
                            trs = []
                            for j in range(4):
                                j0 = g * 4 + j
                                pslot = (j0 % 8) * 256
                                tr = s.pe(lambda e, oc=oc, slot=slot, j0=j0, pslot=pslot: e.transpose(
                                    ps_tr[:, pslot:pslot + 128],
                                    v_scr[:, slot * T + j0 * 128: slot * T + (j0 + 1) * 128],
                                    idn[:]),
                                    deps=vscr_ops[oc] + [ld["idn"]] + tr_war.get(j0 % 8, []))
                                trs.append(tr)
                            base = (g * 4) % 8
                            cp = s.dv(lambda e, oc=oc, g=g, base=base: e.tensor_copy(
                                vt_r[:, g * 4:(g + 1) * 4, oc * 128:(oc + 1) * 128],
                                ps_tr[:, base * 256: base * 256 + 1024]
                                .rearrange("p (j c) -> p j c", c=256)[:, :, 0:128]), deps=trs)
                            for j in range(4):
                                tr_war[(g * 4 + j) % 8] = [cp]
                            vtok_ready[(oc, g)] = cp
                            war_new.append(cp)
                        vscr_war[slot] = war_new
                    return ret

                proj_fm(d_wqkv, 2 * D, xnv, lambda ci, tt: [applyB_tt[(ci, tt)]], ev_vb, ps_proj)

            vs_es.close()

            n_es = contextlib.ExitStack()
            negM = n_es.enter_context(nc.sbuf_tensor(_nm("negM"), [128, 16], F32))
            mrow = n_es.enter_context(nc.sbuf_tensor(_nm("mrow"), [16, 4], F32))
            mrow_bf = n_es.enter_context(nc.sbuf_tensor(_nm("mrow_bf"), [16, 1], BF16))
            nrow_bf = n_es.enter_context(nc.sbuf_tensor(_nm("nrow_bf"), [1, 16], BF16))
            with contextlib.ExitStack() as les:
                ps_m = les.enter_context(nc.psum_tensor(_nm("ps_m"), [16, 2048], F32))
                ps_n1 = les.enter_context(nc.psum_tensor(_nm("ps_n1"), [1, 1024], BF16))
                ps_n2 = les.enter_context(nc.psum_tensor(_nm("ps_n2"), [128, 512], F32))
                red_ops = []
                for qk in range(2):
                    view = qv if qk == 0 else kv
                    vops = q_ops if qk == 0 else k_ops
                    mm_all = []
                    sq_prev = []
                    for cc in range(CH):
                        sqo = s.ac(lambda e, cc=cc, view=view: e.activation(
                            sq_scr[:, 0:T], view(cc), AF.Square),
                            deps=[vops[(cc, tt)] for tt in range(4)] + sq_prev)
                        mms = []
                        for ip in range(4):
                            mms.append(s.pe(lambda e, cc=cc, ip=ip: e.matmul(
                                ps_m[:, ip * 512:(ip + 1) * 512],
                                Gind[:, 14 - 2 * cc:30 - 2 * cc],
                                sq_scr[:, ip * 512:(ip + 1) * 512],
                                start=(cc == 0), stop=(cc == CH - 1)),
                                deps=[sqo, ld["Gind"]] + red_ops
                                + (gate["ps"] if cc == 0 else [])))
                        sq_prev = mms
                        mm_all = mms
                    red = s.dv(lambda e, qk=qk: e.reduce_max(
                        mrow[:, qk:qk + 1], ps_m[:, 0:2048], axis=AX.X), deps=mm_all)
                    sq = s.ac(lambda e, qk=qk: e.activation(
                        mrow[:, 2 + qk:3 + qk], mrow[:, qk:qk + 1], AF.Sqrt), deps=[red])
                    red_ops = [red, sq]
                nm = s.dv(lambda e: e.tensor_scalar(
                    mrow[:, 0:1], mrow[:, 2:3], mrow[:, 3:4], -1.0, ALU.mult, ALU.mult),
                    deps=red_ops)
                nmb = s.dv(lambda e: e.tensor_copy(mrow_bf[:], mrow[:, 0:1]), deps=[nm])
                trn = s.pe(lambda e: e.transpose(ps_n1[0:1, 0:16], mrow_bf[:],
                                                 idn[0:16, 0:16]),
                           deps=[nmb, ld["idn"]] + list(vtok_ready.values()))
                nrw = s.dv(lambda e: e.tensor_copy(nrow_bf[:], ps_n1[0:1, 0:16]), deps=[trn])
                bco = []
                for h in range(16):
                    bm = s.pe(lambda e, h=h: e.matmul(
                        ps_n2[:, h:h + 1], ones_rb[:], nrow_bf[0:1, h:h + 1],
                        start=True, stop=True), deps=[nrw, c_rb])
                    bco.append(bm)
                negM_op = s.dv(lambda e: e.tensor_copy(negM[:], ps_n2[:, 0:16]), deps=bco)
                gate["ps"] = [negM_op, nrw] + red_ops

            at_es = contextlib.ExitStack()
            Dbuf = at_es.enter_context(nc.sbuf_tensor(_nm("Dbuf"), [128, 4 * T], BF16))
            Ebuf = at_es.enter_context(nc.sbuf_tensor(_nm("Ebuf"), [128, 4096], BF16))
            Ebuf2 = at_es.enter_context(nc.sbuf_tensor(_nm("Ebuf2"), [128, 4096], BF16))
            vhat = at_es.enter_context(nc.sbuf_tensor(_nm("vhat"), [128, 32 * 66], BF16))
            vhat_i32 = vhat.ap().bitcast(mybir.dt.int32).rearrange(
                "p (j c) -> p j c", c=33)
            vt_i32 = v_tok.ap().bitcast(mybir.dt.int32).rearrange(
                "p (j c) -> p j c", c=512)
            rsum = [[ln_scr[64 * ln:64 * ln + 1, r * 1024:(r + 1) * 1024]
                     for r in range(2)] for ln in range(2)]
            rlbf = [[sq_scr[0:1, ln * 1024:(ln + 1) * 1024]
                     for r in range(2)] for ln in range(2)]
            rlbc = [p_scr[ln * 64:(ln + 1) * 64, 2048:3072] for ln in range(2)]
            vtr = list(vtok_ready.values())
            ld_D = {}
            d_war = {0: [], 1: []}

            def dview(cc, lane):
                return Dbuf[:, ((cc % 2) * 2 + lane) * T:((cc % 2) * 2 + lane) * T + T]
            vh_ms = s.dv(lambda e: e.memset(vhat[:], 1.0))
            out_ready = {}
            marks_w = s.mark()
            with contextlib.ExitStack() as les:
                psS = [les.enter_context(nc.psum_tensor(_nm("ps_Sa"), [128, 1024], F32)),
                       les.enter_context(nc.psum_tensor(_nm("ps_Sb"), [128, 1024], F32))]
                psU = [les.enter_context(nc.psum_tensor(_nm("ps_Ua"), [65, 1024], F32)),
                       les.enter_context(nc.psum_tensor(_nm("ps_Ub"), [65, 1024], F32))]
                S_war = {ln: list(gate["ps"]) for ln in range(2)}
                E_war = {(ln, sl): [] for ln in range(2) for sl in range(2)}
                E2_war = {(ln, sl): [] for ln in range(2) for sl in range(2)}
                stt_war = {(ln, sl): [] for ln in range(2) for sl in range(2)}
                U_war = {0: list(gate["ps"]), 1: list(gate["ps"])}
                vh_war = {0: [vh_ms], 1: [vh_ms]}
                vcp = {}
                rsum_war = {(ln, r): [] for ln in range(2) for r in range(2)}
                rlbf_war = {ln: [] for ln in range(2)}
                rlbc_war = {0: [], 1: []}
                defer = {0: None, 1: None}
                it_idx = [0]

                def emit_deferred(lane):
                    if defer[lane] is not None:
                        defer[lane]()
                        defer[lane] = None

                def mk_defer(cc, lane, ihalf, smc, ev0, ring):
                    def emit():
                        rs_v = rsum[lane][ring]
                        rb_v = rlbf[lane][ring]
                        ln0 = s.ac(lambda e, rs_v=rs_v: e.activation(
                            rs_v, rs_v, AF.Ln), deps=[smc])
                        rex = s.ac(lambda e, rs_v=rs_v: e.activation(
                            rs_v, rs_v, AF.Exp, scale=-1.0), deps=[ln0])
                        rlb = s.dv(lambda e, rs_v=rs_v, rb_v=rb_v: e.tensor_copy(
                            rb_v, rs_v), deps=[rex] + rlbf_war[lane])
                        rsum_war[(lane, ring)] = [rlb]
                        bms = []
                        for ip in range(2):
                            bm = s.pe(lambda e, lane=lane, ip=ip, rb_v=rb_v: e.matmul(
                                psU[lane][0:64, ip * 512:(ip + 1) * 512],
                                ones_rb[0:1, 0:64], rb_v[0:1, ip * 512:(ip + 1) * 512],
                                start=True, stop=True),
                                deps=[rlb, c_rb, ev0, smc] + rlbc_war[lane])
                            bms.append(bm)
                        rlbf_war[lane] = bms
                        rbe = s.dv(lambda e, lane=lane: e.tensor_copy(
                            rlbc[lane][:], psU[lane][0:64, 0:1024]), deps=bms)
                        U_war[lane] = U_war[lane] + [rbe]
                        i0 = ihalf * 1024
                        mul = s.dv(lambda e, cc=cc, lane=lane, i0=i0: e.tensor_tensor(
                            qv(cc)[lane * 64:(lane + 1) * 64, i0:i0 + 1024],
                            qv(cc)[lane * 64:(lane + 1) * 64, i0:i0 + 1024],
                            rlbc[lane][:], ALU.mult), deps=[rbe, ev0])
                        rlbc_war[lane] = [mul]
                        out_ready[(cc, lane, ihalf)] = mul
                    return emit

                for cc in range(CH):
                    if cc not in ld_D:
                        slot = cc % 2
                        dl_ = []
                        for lane in range(2):
                            h = 2 * cc + lane
                            dl_.append(s.dma_gp(lambda e, h=h, lane=lane, slot=slot: e.dma_start(
                                out=Dbuf[:, (slot * 2 + lane) * T:(slot * 2 + lane) * T + T],
                                in_=d_Dm[h * 128:(h + 1) * 128, :]),
                                f"dm{slot}{lane}", deps=d_war[slot] + (vtr if cc < 2 else [])))
                        ld_D[cc] = dl_
                    for lane in range(2):
                        h = 2 * cc + lane
                        vcp[(cc, lane)] = s.dv(lambda e, h=h, lane=lane: e.tensor_copy(
                            vhat_i32[:, lane * 16:(lane + 1) * 16, 0:32],
                            vt_i32[:, :, h * 32:h * 32 + 32]),
                            deps=[vtok_ready[(cc, g)] for g in range(4)] + vh_war[lane])
                        vh_war[lane] = []
                    for ihalf in range(2):
                        i0 = ihalf * 1024
                        nj = 8 if ihalf == 0 else 16
                        pend = {0: None, 1: None}
                        pv_ops = {0: [], 1: []}
                        for j0 in range(nj):
                            jb = 128 * j0
                            lo0 = max(jb, i0)
                            wtot = i0 + 1024 - lo0
                            sl = j0 % 2
                            smm = {}
                            for lane in range(2):
                                mlist = []
                                for kk in range(2):
                                    blo = max(lo0, i0 + kk * 512)
                                    bhi = i0 + (kk + 1) * 512
                                    if blo >= bhi:
                                        continue
                                    mlist.append(s.pe(lambda e, cc=cc, lane=lane, jb=jb,
                                                      blo=blo, bhi=bhi, i0=i0: e.matmul(
                                        psS[lane][:, blo - i0:bhi - i0],
                                        kv(cc)[lane * 64:(lane + 1) * 64, jb:jb + 128],
                                        qv(cc)[lane * 64:(lane + 1) * 64, blo:bhi],
                                        start=True, stop=True, tile_position=(lane * 64, 0)),
                                        deps=[k_ops[(cc, jb // 512)], q_ops[(cc, blo // 512)],
                                              q_ops[(cc, (bhi - 1) // 512)]] + S_war[lane]))
                                smm[lane] = mlist
                            if j0 == 1:
                                emit_deferred(0)
                                emit_deferred(1)
                            for lane in range(2):
                                if pend[lane] is not None:
                                    pend[lane]()
                                    pend[lane] = None
                            for lane in range(2):
                                h = 2 * cc + lane
                                ebase = (lane * 2 + sl) * 1024
                                ex = s.ac(lambda e, ebase=ebase, wtot=wtot, h=h, lane=lane,
                                          lo0=lo0, i0=i0: e.activation(
                                    Ebuf[:, ebase:ebase + wtot],
                                    psS[lane][:, lo0 - i0:1024],
                                    AF.Exp, bias=negM[:, h:h + 1]),
                                    deps=smm[lane] + [negM_op] + E_war[(lane, sl)])
                                S_war[lane] = [ex]
                                dm = s.dv(lambda e, ebase=ebase, wtot=wtot, cc=cc, lane=lane,
                                          lo0=lo0, jb=jb: e.tensor_tensor(
                                    Ebuf2[:, ebase:ebase + wtot],
                                    Ebuf[:, ebase:ebase + wtot],
                                    dview(cc, lane)[:, lo0 - jb:lo0 - jb + wtot], ALU.mult),
                                    deps=[ex] + ld_D[cc] + E2_war[(lane, sl)])
                                stt_war[(lane, sl)] = [dm]
                                E_war[(lane, sl)] = [dm]

                                def mk_pv(lane=lane, j0=j0, sl=sl, lo0=lo0, ex=dm,
                                          asmop=vcp[(cc, lane)], i0=i0, nj=nj,
                                          ebase=ebase, cc=cc):
                                    def emit():
                                        outp = []
                                        for kk in range(2):
                                            blo = max(lo0, i0 + kk * 512)
                                            bhi = i0 + (kk + 1) * 512
                                            if blo >= bhi:
                                                continue
                                            lastj = min(nj - 1, (bhi - 1) // 128)
                                            pv = s.pe(lambda e, lane=lane, blo=blo, bhi=bhi,
                                                      i0=i0, sl=sl, j0=j0, lastj=lastj,
                                                      lo0=lo0, ebase=ebase: e.matmul(
                                                psU[lane][:, blo - i0:bhi - i0],
                                                vhat[:, (lane * 16 + j0) * 66:
                                                     (lane * 16 + j0) * 66 + 65],
                                                Ebuf2[:, ebase + blo - lo0:ebase + bhi - lo0],
                                                start=(j0 == 0), stop=(j0 == lastj)),
                                                deps=[ex, asmop] + (U_war[lane] if j0 == 0 else []))
                                            outp.append(pv)
                                        pv_ops[lane] += outp
                                        E2_war[(lane, sl)] = outp
                                        vh_war[lane] = vh_war[lane] + outp
                                    return emit
                                pend[lane] = mk_pv()
                        for lane in range(2):
                            if pend[lane] is not None:
                                pend[lane]()
                                pend[lane] = None
                        if ihalf == 1:
                            d_war[cc % 2] = [op for lst in (stt_war[(0, 0)], stt_war[(0, 1)],
                                                            stt_war[(1, 0)], stt_war[(1, 1)]) for op in lst]
                        ring = it_idx[0] % 2
                        it_idx[0] += 1
                        for lane in range(2):
                            ev0 = s.dv(lambda e, cc=cc, lane=lane, i0=i0: e.tensor_copy(
                                qv(cc)[lane * 64:(lane + 1) * 64, i0:i0 + 1024],
                                psU[lane][0:64, 0:1024]),
                                deps=pv_ops[lane])
                            smc = s.dv(lambda e, lane=lane, ring=ring: e.tensor_copy(
                                rsum[lane][ring], psU[lane][64:65, 0:1024]),
                                deps=pv_ops[lane] + rsum_war[(lane, ring)])
                            U_war[lane] = [ev0, smc]
                            defer[lane] = mk_defer(cc, lane, ihalf, smc, ev0, ring)
                emit_deferred(0)
                emit_deferred(1)
                gate["ps"] = (U_war[0] + U_war[1] + S_war[0] + S_war[1]
                              + rlbc_war[0] + rlbc_war[1])
            at_es.close()
            n_es.close()

            res3_dma = {}

            def rhs_out_deps(ci, tt):
                return [out_ready[(ci, lane, tt // 2)] for lane in range(2)]

            with contextlib.ExitStack() as les:
                ps_proj = les.enter_context(nc.psum_tensor(_nm("ps_wo2"), [128, 4096], F32))
                x1sc = les.enter_context(nc.sbuf_tensor(_nm("x1sc"), [128, 2 * T], F32))
                x3sc = les.enter_context(nc.sbuf_tensor(_nm("x3sc"), [128, 2 * T], F32))
                xdma = {}
                stt3 = {}
                x3prev = {}

                def ev_wo2(oc, tt, pp, mml, par):
                    slot = oc % 2
                    chan = f"str{slot}"
                    chan3 = f"xst{slot}"
                    if oc not in xdma:
                        xdma[oc] = s.dma_gp(lambda e, oc=oc, slot=slot: e.dma_start(
                            out=x1sc[:, slot * T:slot * T + T],
                            in_=d_xs1[oc * 128:(oc + 1) * 128, :]), chan,
                            deps=g_chan.get(chan, []) + gate["ps"])
                        g_chan[chan] = []
                    ev = s.dv(lambda e, oc=oc, tt=tt, pp=pp, slot=slot: e.scalar_tensor_tensor(
                        x3sc[:, slot * T + tt * 512: slot * T + (tt + 1) * 512], pp,
                        boutc[:, oc:oc + 1],
                        x1sc[:, slot * T + tt * 512: slot * T + (tt + 1) * 512],
                        ALU.add, ALU.add),
                        deps=[mml, xdma[oc], ld["boutc"], x3prev.get(slot)]
                        + ([] if slot in x3prev else gate["ps"]))
                    g_chan[chan].append(ev)
                    stt3.setdefault(oc, []).append(ev)
                    if tt == 3:
                        dmo = s.dma_gp(lambda e, oc=oc, slot=slot: e.dma_start(
                            out=d_xs2[oc * 128:(oc + 1) * 128, :],
                            in_=x3sc[:, slot * T:slot * T + T]), chan3,
                            deps=stt3[oc] + g_chan.get(chan3, []))
                        g_chan[chan3] = []
                        x3prev[slot] = dmo
                        res3_dma[oc] = dmo
                    return [ev]

                proj_fm(dw["wout"], 0, qv, rhs_out_deps, ev_wo2, ps_proj, nreg=4,
                        pre_marks=marks_w, pre_dg2=2)
            qk_es.close()

            x3_ld = {}
            for cc in range(CH):
                x3_ld[cc] = s.dma_gp(lambda e, cc=cc: e.dma_start(
                    out=arv(cc), in_=d_xs2[cc * 128:(cc + 1) * 128, :]),
                    f"xld{cc}", deps=[res3_dma[cc]] + rhs_out_deps(cc, 0)
                    + rhs_out_deps(cc, 2))

            def x3_ready(cc):
                return [x3_ld[cc]]
        else:
            x3_ready = x2_ready

        x4_ready = ffn_layer(x3_ready) if stop_after >= 4 else x3_ready

        out_dmas = []
        for cc in range(CH):
            od = s.dma_gp(lambda e, cc=cc: e.dma_start(
                out=d_out[cc * 128:(cc + 1) * 128, :], in_=arv(cc)),
                "out", deps=x4_ready(cc), bulk=True)
            out_dmas.append(od)

        s.assign(sems, mk_dma_sem)
        s.final.append(("gp", out_dmas[-1]))
        with nc.Block() as block:
            @block.tensor
            def _(eng):
                s.emit("pe", eng)

            @block.vector
            def _(eng):
                s.emit("dv", eng)

            @block.scalar
            def _(eng):
                s.emit("ac", eng)

            @block.gpsimd
            def _(eng):
                s.emit("gp", eng)

            @block.sync
            def _(eng):
                s.emit("sq", eng)

    return nc


def _host_inputs(inputs):
    f32 = np.float32
    x = np.ascontiguousarray(np.transpose(inputs["L0_attn_a_x"], (0, 2, 1)))
    ins = {}
    ins["wqT"] = np.ascontiguousarray(inputs["L0_attn_a_Wq"].T)
    ins["wvT"] = np.ascontiguousarray(inputs["L0_attn_a_Wv"].T)
    ins["wgT"] = np.ascontiguousarray(inputs["L0_attn_a_Wg"].T)
    ins["woT"] = np.ascontiguousarray(inputs["L0_attn_a_Wo"].T)
    ins["wrT"] = np.ascontiguousarray(inputs["L0_ffn_W_r"].T)
    ins["wkT"] = np.ascontiguousarray(inputs["L0_ffn_W_k"].T)
    ins["wfvT"] = np.ascontiguousarray(inputs["L0_ffn_W_v"].T)
    ins["wout"] = np.ascontiguousarray(inputs["L0_attn_b_w_out"])
    ins["wqkv"] = np.ascontiguousarray(inputs["L0_attn_b_w_qkv"])

    def cols(v):
        return np.ascontiguousarray(np.asarray(v, dtype=f32).reshape(CH, 128).T)

    ins["bout_c"] = cols(inputs["L0_attn_b_b_out"])
    ins["mur_c"] = cols(np.asarray(inputs["L0_ffn_mu_r"], dtype=f32) - 1.0)
    ins["muk_c"] = cols(np.asarray(inputs["L0_ffn_mu_k"], dtype=f32) - 1.0)
    ins["gam_c"] = cols(inputs["L0_attn_a_ln_gamma"])
    ins["bet_c"] = cols(inputs["L0_attn_a_ln_beta"])
    beta = np.asarray(inputs["L0_attn_a_beta"], dtype=f32)
    p = np.arange(128)
    ccg = np.arange(CH)
    ins["beta_pc"] = np.ascontiguousarray(beta[(2 * ccg[None, :] + (p[:, None] >= 64))])
    pp = np.arange(128)[:, None]
    cc_ = np.arange(T)[None, :]
    rel = np.where(cc_ >= pp, -(cc_ - pp).astype(np.float64), NEGBIG).astype(f32)
    ins["relmaster"] = np.ascontiguousarray(rel)
    import ml_dtypes
    slopes = _slopes16()
    Dm = np.zeros((NH * 128, T), dtype=ml_dtypes.bfloat16)
    for h in range(NH):
        Dm[h * 128:(h + 1) * 128, :] = np.exp(slopes[h] * rel.astype(np.float64)).astype(ml_dtypes.bfloat16)
    ins["Dmaster"] = Dm
    G = np.zeros((128, 32), dtype=ml_dtypes.bfloat16)
    G[0:64, 14] = 1
    G[64:128, 15] = 1
    ins["Gind"] = G
    ins["idn"] = np.eye(128, dtype=ml_dtypes.bfloat16)
    return x, ins


def kernel(**inputs):
    x, ins = _host_inputs(inputs)
    nc = build(STOP_AFTER)
    in_maps = []
    for b in range(8):
        m = dict(ins)
        m["xT"] = np.ascontiguousarray(x[b])
        in_maps.append(m)
    res = run_bass_kernel_spmd(nc, in_maps, core_ids=list(range(8)))
    out = np.stack([np.asarray(res.results[b]["out"]) for b in range(8)], axis=0)
    return np.ascontiguousarray(out.transpose(0, 2, 1)).astype(np.float32)


# version 21
# speedup vs baseline: 1.1810x; 1.1810x over previous
import contextlib
import numpy as np
import concourse.bass as bass
import concourse.mybir as mybir
from concourse.bass_utils import run_bass_kernel_spmd

F32 = mybir.dt.float32
BF16 = mybir.dt.bfloat16
AF = mybir.ActivationFunctionType
ALU = mybir.AluOpType
AX = mybir.AxisListType

T = 2048
HF = 1024
D = 1024
CH = 8
NH = 16
LN_EPS = 1e-5
QSCALE = 0.125
NEGBIG = -1e30
STOP_AFTER = 4


def _slopes16():
    start = 2.0 ** (-(2.0 ** -1.0))
    return [start * start ** i for i in range(16)]


QUEUES = ("pe", "dv", "ac", "gp", "sq")


class Op:
    __slots__ = ("q", "fn", "deps", "dma", "inc", "tick", "sem", "chan", "bulk")

    def __init__(self, q, fn, deps, dma, chan=None, bulk=False):
        self.q = q
        self.fn = fn
        self.deps = [d for d in deps if d is not None]
        self.dma = dma
        self.inc = dma
        self.tick = None
        self.sem = None
        self.chan = chan
        self.bulk = bulk


class Sched:
    def __init__(self):
        self.seq = {q: [] for q in QUEUES}
        self.final = []
        self._ins = None

    def _op(self, q, fn, deps, dma=False, chan=None, bulk=False):
        o = Op(q, fn, deps, dma, chan, bulk)
        for d in o.deps:
            d.inc = True
        if self._ins is not None:
            self.seq[q].insert(self._ins[q], o)
            self._ins[q] += 1
        else:
            self.seq[q].append(o)
        return o

    def mark(self):
        return {q: len(self.seq[q]) for q in QUEUES}

    @contextlib.contextmanager
    def at_marks(self, marks):
        prev = self._ins
        self._ins = marks
        try:
            yield
        finally:
            self._ins = prev

    def pe(self, fn, deps=()):
        return self._op("pe", fn, deps)

    def dv(self, fn, deps=()):
        return self._op("dv", fn, deps)

    def ac(self, fn, deps=()):
        return self._op("ac", fn, deps)

    def gp(self, fn, deps=()):
        return self._op("gp", fn, deps)

    def dma_sq(self, fn, chan, deps=(), bulk=False):
        return self._op("sq", fn, deps, dma=True, chan=chan, bulk=bulk)

    def dma_gp(self, fn, chan, deps=(), bulk=False):
        return self._op("gp", fn, deps, dma=True, chan=chan, bulk=bulk)

    def assign(self, sems, mk_dma_sem):
        counters = {k: 0 for k in sems}
        dma_sems = {}
        dma_counters = {}
        bulk_ops = {}
        for q in QUEUES:
            for o in self.seq[q]:
                if not o.inc:
                    continue
                if o.dma:
                    if o.chan not in dma_sems:
                        dma_sems[o.chan] = mk_dma_sem(o.chan)
                        dma_counters[o.chan] = 0
                    dma_counters[o.chan] += 16
                    o.tick = dma_counters[o.chan]
                    o.sem = dma_sems[o.chan]
                    if o.bulk:
                        bulk_ops.setdefault(o.chan, []).append(o)
                else:
                    counters[o.q] += 1
                    o.tick = counters[o.q]
                    o.sem = sems[o.q]
        for chan, ops in bulk_ops.items():
            for o in ops:
                o.tick = dma_counters[chan]

    def emit(self, q, eng):
        hw = {}
        for o in self.seq[q]:
            for d in o.deps:
                key = id(d.sem)
                if hw.get(key, -1) < d.tick:
                    eng.wait_ge(d.sem, d.tick)
                    hw[key] = d.tick
            inst = o.fn(eng)
            if o.inc:
                inst.then_inc(o.sem, 16 if o.dma else 1)
        for (fq, o) in self.final:
            if fq == q:
                eng.wait_ge(o.sem, o.tick)


def build(stop_after=4, use_recip=False, dm_gp=True):
    nc = bass.Bass("TRN2", target_bir_lowering=False)
    s = Sched()
    slopes = _slopes16()
    _ncnt = [0]

    def _nm(b):
        _ncnt[0] += 1
        return f"{b}{_ncnt[0]}"

    d_x = nc.dram_tensor("xT", [D, T], F32, kind="ExternalInput").ap()
    dw = {}
    for nm in ("wqT", "wvT", "wgT", "woT", "wrT", "wkT", "wfvT", "wout"):
        dw[nm] = nc.dram_tensor(nm, [D, D], F32, kind="ExternalInput").ap()
    d_wqkv = nc.dram_tensor("wqkv", [D, 3 * D], F32, kind="ExternalInput").ap()
    d_bout = nc.dram_tensor("bout_c", [128, CH], F32, kind="ExternalInput").ap()
    d_mur = nc.dram_tensor("mur_c", [128, CH], F32, kind="ExternalInput").ap()
    d_muk = nc.dram_tensor("muk_c", [128, CH], F32, kind="ExternalInput").ap()
    d_gam = nc.dram_tensor("gam_c", [128, CH], F32, kind="ExternalInput").ap()
    d_bet = nc.dram_tensor("bet_c", [128, CH], F32, kind="ExternalInput").ap()
    d_betapc = nc.dram_tensor("beta_pc", [128, CH], F32, kind="ExternalInput").ap()
    d_rel = nc.dram_tensor("relmaster", [128, T], F32, kind="ExternalInput").ap()
    d_Dm = nc.dram_tensor("Dmaster", [NH * 128, T], BF16, kind="ExternalInput").ap()
    d_G = nc.dram_tensor("Gind", [128, 32], BF16, kind="ExternalInput").ap()
    d_idn = nc.dram_tensor("idn", [128, 128], BF16, kind="ExternalInput").ap()
    d_out = nc.dram_tensor("out", [D, T], F32, kind="ExternalOutput").ap()
    d_xs1 = nc.dram_tensor("xscr1", [D, T], F32, kind="Internal").ap()
    d_xs2 = nc.dram_tensor("xscr2", [D, T], F32, kind="Internal").ap()

    es = contextlib.ExitStack()
    with es:
        sems = {}
        for k in ("pe", "dv", "ac", "gp"):
            sems[k] = es.enter_context(nc.semaphore(k))

        def mk_dma_sem(chan):
            return es.enter_context(nc.semaphore(f"dma_{chan}"))

        xn = es.enter_context(nc.sbuf_tensor("xn", [128, CH * T], BF16))
        arena = es.enter_context(nc.sbuf_tensor("arena", [128, CH * T], F32))
        arena_bf = arena.ap().bitcast(BF16)
        sq_scr = es.enter_context(nc.sbuf_tensor("sq_scr", [128, T], BF16))
        w_stg = es.enter_context(nc.sbuf_tensor("w_stg", [128, 1024], F32))
        w_cache = es.enter_context(nc.sbuf_tensor("w_cache", [128, 4096], BF16))
        ln_scr = es.enter_context(nc.sbuf_tensor("ln_scr", [128, 2048], F32))
        p_scr = es.enter_context(nc.sbuf_tensor("p_scr", [128, 4096], F32))
        o128 = es.enter_context(nc.sbuf_tensor("o128", [128, 128], BF16))
        ones_rb = es.enter_context(nc.sbuf_tensor("ones_rb", [1, 128], BF16))
        lam_c = es.enter_context(nc.sbuf_tensor("lam_c", [128, CH], F32))
        betap = es.enter_context(nc.sbuf_tensor("betap", [128, CH], F32))
        murc = es.enter_context(nc.sbuf_tensor("murc", [128, CH], F32))
        mukc = es.enter_context(nc.sbuf_tensor("mukc", [128, CH], F32))
        gamc = es.enter_context(nc.sbuf_tensor("gamc", [128, CH], F32))
        betc = es.enter_context(nc.sbuf_tensor("betc", [128, CH], F32))
        boutc = es.enter_context(nc.sbuf_tensor("boutc", [128, CH], F32))
        Gind = es.enter_context(nc.sbuf_tensor("Gind_sb", [128, 32], BF16))
        idn = es.enter_context(nc.sbuf_tensor("idn_sb", [128, 128], BF16))

        def xnv(cc):
            return xn[:, cc * T:(cc + 1) * T]

        def arv(cc):
            return arena[:, cc * T:(cc + 1) * T]

        c_o128 = s.dv(lambda e: e.memset(o128[:], 1.0))
        c_rb = s.dv(lambda e: e.memset(ones_rb[:], 1.0))
        ld = {}
        for key, dst, src in (("betap", betap, d_betapc), ("murc", murc, d_mur),
                              ("mukc", mukc, d_muk), ("gamc", gamc, d_gam),
                              ("betc", betc, d_bet), ("boutc", boutc, d_bout),
                              ("Gind", Gind, d_G), ("idn", idn, d_idn)):
            ld[key] = s.dma_gp(lambda e, dst=dst, src=src: e.dma_start(out=dst[:], in_=src),
                               "const", bulk=True)
        lam_op = s.ac(lambda e: e.activation(lam_c[:], betap[:], AF.Sigmoid), deps=[ld["betap"]])

        g_stg = {0: [], 1: [], 2: [], 3: []}
        g_cache = {0: [], 1: []}
        gate = {"ps": []}
        g_chan = {}

        def mk_dram_src(dram_ap, buf):
            state = {"cnt": 0}

            def src(cc, half):
                slot = state["cnt"] % 4
                state["cnt"] += 1
                chan = f"xa{slot}"
                emit = s.dma_sq if slot < 2 else s.dma_gp
                d = emit(lambda e, cc=cc, half=half, slot=slot: e.dma_start(
                    out=buf[:, slot * HF:slot * HF + HF],
                    in_=dram_ap[cc * 128:(cc + 1) * 128, half * HF:half * HF + HF]),
                    chan, deps=g_chan.get(chan, []))
                marks = []
                g_chan[chan] = marks
                return buf[:, slot * HF:slot * HF + HF], [d], marks
            return src

        def mk_sbuf_src(view_fn, deps_fn):
            def src(cc, half):
                return view_fn(cc)[:, half * HF:half * HF + HF], list(deps_fn(cc)), []
            return src

        def emit_ln(src_fn, bcs, bch, cast_dst_fn, apply_dst_fn=None,
                    apply_from_cast=False, extra_src_deps_fn=None,
                    apply_gp_cut=6):
            apply_ops = [[] for _ in range(CH)]
            apply_tt = {}
            bc_done = []
            csts = {}
            with contextlib.ExitStack() as les:
                ps_sum = les.enter_context(nc.psum_tensor(_nm("ps_su"), [128, HF], F32))
                ps_sq = les.enter_context(nc.psum_tensor(_nm("ps_sq"), [128, HF], F32))
                stats_war = list(gate["ps"])
                sq_war = {0: [], 1: []}
                apply_war = {i: [] for i in range(4)}
                for half in range(2):
                    h0 = half * HF
                    mm_sum_last = []
                    mm_sq_last = []
                    for cc in range(CH):
                        src, sdeps, marks = src_fn(cc, half)
                        if extra_src_deps_fn is not None:
                            sdeps = sdeps + extra_src_deps_fn(half)
                        cdst = cast_dst_fn(cc)
                        so = (cc % 2) * HF
                        if cc % 2 == 0:
                            cst = s.ac(lambda e, src=src, cdst=cdst, h0=h0: e.activation(
                                cdst[:, h0:h0 + HF], src, AF.Identity), deps=sdeps)
                            sqc = s.dv(lambda e, src=src, so=so: e.tensor_tensor(
                                sq_scr[:, so:so + HF], src, src, ALU.mult),
                                deps=sdeps + sq_war[cc % 2])
                        else:
                            cst = s.dv(lambda e, src=src, cdst=cdst, h0=h0: e.tensor_copy(
                                cdst[:, h0:h0 + HF], src), deps=sdeps)
                            sqc = s.ac(lambda e, src=src, so=so: e.activation(
                                sq_scr[:, so:so + HF], src, AF.Square),
                                deps=sdeps + sq_war[cc % 2])
                        marks += [cst, sqc]
                        csts[(cc, half)] = cst
                        mm_q = []
                        mm_s = []
                        for ip in range(2):
                            mm_s.append(s.pe(lambda e, cdst=cdst, h0=h0, ip=ip, cc=cc: e.matmul(
                                ps_sum[:, ip * 512:(ip + 1) * 512], o128[:],
                                cdst[:, h0 + ip * 512:h0 + (ip + 1) * 512],
                                start=(cc == 0), stop=(cc == CH - 1)),
                                deps=[cst, c_o128] + (stats_war if cc == 0 else [])))
                            mm_q.append(s.pe(lambda e, so=so, ip=ip, cc=cc: e.matmul(
                                ps_sq[:, ip * 512:(ip + 1) * 512], o128[:],
                                sq_scr[:, so + ip * 512:so + (ip + 1) * 512],
                                start=(cc == 0), stop=(cc == CH - 1)),
                                deps=[sqc, c_o128] + (stats_war if cc == 0 else [])))
                        sq_war[cc % 2] = mm_q
                        mm_sum_last = mm_s
                        mm_sq_last = mm_q
                    rs = bcs[:, h0:h0 + HF]
                    rh = bch[:, h0:h0 + HF]
                    nm_v = ln_scr[:, h0:h0 + HF]
                    o1 = s.dv(lambda e, nm_v=nm_v: e.tensor_scalar_mul(
                        nm_v, ps_sum[:, 0:HF], -1.0 / D), deps=mm_sum_last)
                    o2 = s.dv(lambda e, rh=rh: e.tensor_scalar(
                        rh, ps_sq[:, 0:HF], 1.0 / D, LN_EPS, ALU.mult, ALU.add),
                        deps=mm_sq_last)
                    o3 = s.dv(lambda e, nm_v=nm_v: e.tensor_tensor(
                        ps_sum[:, 0:HF], nm_v, nm_v, ALU.mult), deps=[o1])
                    o4 = s.dv(lambda e, rh=rh: e.tensor_tensor(
                        rh, rh, ps_sum[:, 0:HF], ALU.subtract), deps=[o2, o3])
                    o5 = s.ac(lambda e, rh=rh: e.activation(
                        ps_sq[:, 0:HF], rh, AF.Ln), deps=[o4])
                    o6 = s.ac(lambda e, rs=rs: e.activation(
                        rs, ps_sq[:, 0:HF], AF.Exp, scale=-0.5), deps=[o5])
                    o7 = s.dv(lambda e, rh=rh, nm_v=nm_v, rs=rs: e.tensor_tensor(
                        rh, nm_v, rs, ALU.mult), deps=[o6, o4])
                    stats_war = [o4, o6]
                    bc_done += [o6, o7]
                    if apply_dst_fn is not None:
                        for cc in range(CH):
                            if apply_from_cast:
                                aview0 = cast_dst_fn(cc)[:, h0:h0 + HF]
                                adeps = [csts[(cc, half)]]
                            else:
                                aview0, adeps, _m = src_fn(cc, half)
                            for q in range(2):
                                q0 = q * 512
                                aview = aview0[:, q0:q0 + 512]
                                eng = s.dv if cc < apply_gp_cut else s.gp
                                slot = (cc * 2 + q) % 4
                                sB = p_scr[:, slot * 512:slot * 512 + 512]
                                t1 = eng(lambda e, aview=aview, sB=sB, rs=rs, q0=q0:
                                         e.tensor_tensor(
                                    sB, aview, rs[:, q0:q0 + 512], ALU.mult),
                                    deps=adeps + [o6] + apply_war[slot])
                                t2 = eng(lambda e, cc=cc, sB=sB, rh=rh, h0=h0, q0=q0:
                                         e.tensor_tensor(
                                    apply_dst_fn(cc)[:, h0 + q0:h0 + q0 + 512], sB,
                                    rh[:, q0:q0 + 512], ALU.add),
                                    deps=[t1, o7])
                                apply_war[slot] = [t2]
                                apply_ops[cc].append(t2)
                                apply_tt[(cc, half * 2 + q)] = t2
                gate["ps"] = list(stats_war)
            return (apply_ops if apply_dst_fn is not None else None), apply_tt, bc_done

        def proj_fm(wdram, col0, rhs_fn, rhs_deps_fn, evac_fn, ps_proj, war_fn=None,
                    second=None, nreg=2, pre_marks=None, pre_dg2=0):
            out_ops = {}
            cache_users = g_cache
            region_users = {}
            piece = 0
            stg_cnt = 0
            nW = 2 if second is not None else 1
            for dg2 in range(4):
                use_mark = pre_marks is not None and dg2 < pre_dg2
                for wi in range(nW):
                    wd, c0 = (wdram, col0) if wi == 0 else (second[0], second[1])
                    key = (dg2 % 2) if nW == 1 else wi
                    consumed = cache_users[key]
                    cache_users[key] = []
                    for c in range(CH):
                        slot = stg_cnt % 4
                        stg_cnt += 1
                        stv = w_stg[:, slot * 256:slot * 256 + 256]
                        dma = s.dma_sq(lambda e, c=c, dg2=dg2, stv=stv, wd=wd, c0=c0: e.dma_start(
                            out=stv, in_=wd[c * 128:(c + 1) * 128,
                                            c0 + dg2 * 256:c0 + dg2 * 256 + 256]),
                            f"wstg{slot}", deps=g_stg[slot])
                        if nW == 1:
                            dstv = w_cache[:, (dg2 % 2) * 2048 + c * 256:(dg2 % 2) * 2048 + c * 256 + 256]
                        else:
                            dstv = w_cache[:, wi * 2048 + c * 256:wi * 2048 + c * 256 + 256]
                        ctx = s.at_marks(pre_marks) if use_mark else contextlib.nullcontext()
                        with ctx:
                            if (c + dg2) % 2 == 0:
                                cst = s.dv(lambda e, stv=stv, dstv=dstv: e.tensor_copy(dstv, stv),
                                           deps=[dma] + consumed)
                            else:
                                cst = s.ac(lambda e, stv=stv, dstv=dstv: e.activation(
                                    dstv, stv, AF.Identity), deps=[dma] + consumed)
                        g_stg[slot] = [cst]
                        out_ops.setdefault(("cast", (dg2, wi)), {})[c] = cst
                for tt in range(4):
                    par = piece % nreg
                    for wi in range(nW):
                        rfn = rhs_fn if wi == 0 else second[2]
                        rdeps = rhs_deps_fn if wi == 0 else second[3]
                        efn = evac_fn if wi == 0 else second[4]
                        psp = ps_proj if wi == 0 else second[5]
                        mms_dd = {0: [], 1: []}
                        for ci in range(CH):
                            for dd in range(2):
                                pp = psp[:, par * 1024 + dd * 512: par * 1024 + dd * 512 + 512]
                                deps = [out_ops[("cast", (dg2, wi))][ci]] + rdeps(ci, tt)
                                if ci == 0:
                                    deps = deps + region_users.get((wi, par), gate["ps"])
                                    if war_fn is not None and wi == 0:
                                        deps = deps + war_fn(dg2 * 2 + dd, tt)
                                if nW == 1:
                                    wv = w_cache[:, (dg2 % 2) * 2048 + ci * 256 + dd * 128:
                                                 (dg2 % 2) * 2048 + ci * 256 + dd * 128 + 128]
                                else:
                                    wv = w_cache[:, wi * 2048 + ci * 256 + dd * 128:
                                                 wi * 2048 + ci * 256 + dd * 128 + 128]
                                mm = s.pe(lambda e, wv=wv, pp=pp, ci=ci, tt=tt, rfn=rfn: e.matmul(
                                    pp, wv, rfn(ci)[:, tt * 512:(tt + 1) * 512],
                                    start=(ci == 0), stop=(ci == CH - 1)), deps=deps)
                                mms_dd[dd].append(mm)
                                cache_users[(dg2 % 2) if nW == 1 else wi].append(mm)
                        evs_all = []
                        for dd in range(2):
                            oc = dg2 * 2 + dd
                            pp = psp[:, par * 1024 + dd * 512: par * 1024 + dd * 512 + 512]
                            evs = efn(oc, tt, pp, mms_dd[dd][-1], par)
                            out_ops[(wi, oc, tt)] = evs
                            evs_all += evs
                        region_users[(wi, par)] = evs_all
                    piece += 1
            gate["ps"] = [op for v in region_users.values() for op in v]
            return out_ops

        def proj_seq(specs, ps, nreg=4, post_tile=None):
            out_ops = {}
            region_users = {}
            piece = 0
            stg_cnt = 0
            tile_idx = 0
            for dg2 in range(4):
                for si, (wd, c0, rhs_fn, rdeps_fn, evac_fn) in enumerate(specs):
                    key = tile_idx % 2
                    consumed = g_cache[key]
                    g_cache[key] = []
                    casts = {}
                    for c in range(CH):
                        slot = stg_cnt % 4
                        stg_cnt += 1
                        stv = w_stg[:, slot * 256:slot * 256 + 256]
                        dma = s.dma_sq(lambda e, c=c, dg2=dg2, stv=stv, wd=wd, c0=c0:
                                       e.dma_start(
                            out=stv, in_=wd[c * 128:(c + 1) * 128,
                                            c0 + dg2 * 256:c0 + dg2 * 256 + 256]),
                            f"wstg{slot}", deps=g_stg[slot])
                        dstv = w_cache[:, key * 2048 + c * 256:key * 2048 + c * 256 + 256]
                        if (c + tile_idx) % 2 == 0:
                            cst = s.dv(lambda e, stv=stv, dstv=dstv: e.tensor_copy(
                                dstv, stv), deps=[dma] + consumed)
                        else:
                            cst = s.ac(lambda e, stv=stv, dstv=dstv: e.activation(
                                dstv, stv, AF.Identity), deps=[dma] + consumed)
                        g_stg[slot] = [cst]
                        casts[c] = cst
                    mm_tile = []
                    for tt in range(4):
                        par = piece % nreg
                        mms_dd = {0: [], 1: []}
                        for ci in range(CH):
                            for dd in range(2):
                                pp = ps[:, par * 1024 + dd * 512:par * 1024 + dd * 512 + 512]
                                deps = [casts[ci]] + rdeps_fn(ci, tt)
                                if ci == 0:
                                    deps = deps + region_users.get(par, gate["ps"])
                                wv = w_cache[:, key * 2048 + ci * 256 + dd * 128:
                                             key * 2048 + ci * 256 + dd * 128 + 128]
                                mm = s.pe(lambda e, wv=wv, pp=pp, ci=ci, tt=tt,
                                          rhs_fn=rhs_fn: e.matmul(
                                    pp, wv, rhs_fn(ci)[:, tt * 512:(tt + 1) * 512],
                                    start=(ci == 0), stop=(ci == CH - 1)), deps=deps)
                                mms_dd[dd].append(mm)
                                mm_tile.append(mm)
                                g_cache[key].append(mm)
                        evs_all = []
                        for dd in range(2):
                            oc = dg2 * 2 + dd
                            pp = ps[:, par * 1024 + dd * 512:par * 1024 + dd * 512 + 512]
                            evs = evac_fn(oc, tt, pp, mms_dd[dd][-1], par)
                            out_ops[(si, oc, tt)] = evs
                            evs_all += evs
                        region_users[par] = evs_all
                        piece += 1
                    out_ops[("mm", si, dg2)] = mm_tile
                    tile_idx += 1
                    if post_tile is not None:
                        post_tile(si, dg2)
            gate["ps"] = [op for v in region_users.values() for op in v]
            return out_ops

        a_es = contextlib.ExitStack()
        bcs_a = a_es.enter_context(nc.sbuf_tensor(_nm("bcs_a"), [128, T], F32))
        bch_a = a_es.enter_context(nc.sbuf_tensor(_nm("bch_a"), [128, T], F32))
        xsc_a = a_es.enter_context(nc.sbuf_tensor(_nm("xsc_a"), [128, 4 * HF], F32))
        src_a = mk_dram_src(d_x, xsc_a)
        applyA, applyA_tt, _ = emit_ln(src_a, bcs_a, bch_a, cast_dst_fn=xnv,
                                       apply_dst_fn=xnv, apply_from_cast=True,
                                       apply_gp_cut=5)
        a_es.close()

        st_es = contextlib.ExitStack()
        states = st_es.enter_context(nc.sbuf_tensor(_nm("states"), [128, 12288], F32))
        gate_bf = states.ap().bitcast(BF16)

        def stv(cc):
            return states[:, 8192 + (cc % 2) * T:8192 + (cc % 2) * T + T]

        def ygv(cc):
            return gate_bf[:, cc * T:(cc + 1) * T]

        v_ops = {}
        y_ops = {}
        sil_ops = {}
        scan_ops = {}
        with contextlib.ExitStack() as les:
            ps_a = les.enter_context(nc.psum_tensor(_nm("ps_vqg"), [128, 4096], F32))

            def ev_v(oc, tt, pp, mml, par):
                ev = s.ac(lambda e, oc=oc, tt=tt, pp=pp: e.activation(
                    arv(oc)[:, tt * 512:(tt + 1) * 512], pp, AF.Identity), deps=[mml])
                v_ops[(oc, tt)] = ev
                return [ev]

            def ev_q(oc, tt, pp, mml, par):
                ev = s.dv(lambda e, oc=oc, tt=tt, pp=pp: e.tensor_tensor(
                    arv(oc)[:, tt * 512:(tt + 1) * 512], pp,
                    stv(oc)[:, tt * 512:(tt + 1) * 512], ALU.mult),
                    deps=[mml, scan_ops[oc]])
                y_ops[(oc, tt)] = ev
                return [ev]

            def ev_g(oc, tt, pp, mml, par):
                ev = s.ac(lambda e, oc=oc, tt=tt, pp=pp: e.activation(
                    ygv(oc)[:, tt * 512:(tt + 1) * 512], pp, AF.Silu),
                    deps=[mml] + [y_ops[(oc // 2, t4)] for t4 in range(4)])
                sil_ops[(oc, tt)] = ev
                return [ev]

            def post_tile(si, dg2):
                if si == 0:
                    for dd in range(2):
                        oc = dg2 * 2 + dd
                        war = ([y_ops[(oc - 2, tt)] for tt in range(4)]
                               if oc >= 2 else [])
                        scan_ops[oc] = s.dv(lambda e, oc=oc: e.tensor_tensor_scan(
                            stv(oc), lam_c[:, oc:oc + 1].broadcast_to([128, T]),
                            arv(oc), 0.0, ALU.mult, ALU.add),
                            deps=[v_ops[(oc, tt)] for tt in range(4)] + [lam_op] + war)

            seq_ops = proj_seq(
                [(dw["wvT"], 0, xnv, lambda ci, tt: [applyA_tt[(ci, tt)]], ev_v),
                 (dw["wqT"], 0, xnv, lambda ci, tt: [applyA_tt[(ci, tt)]], ev_q),
                 (dw["wgT"], 0, xnv, lambda ci, tt: [applyA_tt[(ci, tt)]], ev_g)],
                ps_a, nreg=4, post_tile=post_tile)

        marks_wo = s.mark()

        yg_es = contextlib.ExitStack()
        bcs_y = yg_es.enter_context(nc.sbuf_tensor(_nm("bcs_y"), [128, T], F32))
        bch_y = yg_es.enter_context(nc.sbuf_tensor(_nm("bch_y"), [128, T], F32))
        src_y = mk_sbuf_src(arv, lambda cc: [y_ops[(cc, tt)] for tt in range(4)])
        all_yops = [y_ops[(c, t)] for c in range(CH) for t in range(4)]
        _, _, bc_y = emit_ln(src_y, bcs_y, bch_y, cast_dst_fn=xnv,
                             extra_src_deps_fn=lambda half: all_yops
                             + seq_ops[("mm", 2, 3)])

        yg_ops = {}
        y_free = {}
        gate_all = []
        for tt in range(4):
            for cc in range(CH):
                sl = slice(tt * 512, (tt + 1) * 512)
                pslot = (tt * CH + cc) % 4
                sB = p_scr[:, (pslot * 2) * 512:(pslot * 2) * 512 + 512]
                sC = p_scr[:, (pslot * 2 + 1) * 512:(pslot * 2 + 1) * 512 + 512]
                t1 = s.dv(lambda e, cc=cc, sl=sl, sB=sB: e.tensor_tensor(
                    sB, arv(cc)[:, sl], bcs_y[:, sl], ALU.mult),
                    deps=[y_ops[(cc, tt)]] + bc_y + y_free.get((pslot, 0), []))
                t2 = s.dv(lambda e, sl=sl, sB=sB, sC=sC: e.tensor_tensor(
                    sC, sB, bch_y[:, sl], ALU.add), deps=[t1] + y_free.get((pslot, 1), []))
                yn = s.ac(lambda e, cc=cc, sB=sB, sC=sC: e.activation(
                    sB, sC, AF.Identity, bias=betc[:, cc:cc + 1], scale=gamc[:, cc:cc + 1]),
                    deps=[t2, ld["gamc"], ld["betc"]])
                yg = s.gp(lambda e, cc=cc, sl=sl, sB=sB: e.tensor_tensor(
                    ygv(cc)[:, sl], sB, ygv(cc)[:, sl], ALU.mult),
                    deps=[yn, sil_ops[(cc, tt)]])
                yg_ops[(cc, tt)] = yg
                y_free[(pslot, 0)] = [yg]
                y_free[(pslot, 1)] = [yn]
                gate_all += [t1, t2]
        yg_es.close()

        res1_ops = {}
        with contextlib.ExitStack() as les:
            ps_proj = les.enter_context(nc.psum_tensor(_nm("ps_wa"), [128, 4096], F32))
            xsc2 = les.enter_context(nc.sbuf_tensor(_nm("xsc2_a"), [128, 2 * T], F32))
            xdma = {}

            def ev_wo(oc, tt, pp, mml, par):
                slot = oc % 2
                chan = f"str{slot}"
                if oc not in xdma:
                    xdma[oc] = s.dma_gp(lambda e, oc=oc, slot=slot: e.dma_start(
                        out=xsc2[:, slot * T:slot * T + T],
                        in_=d_x[oc * 128:(oc + 1) * 128, :]), chan,
                        deps=g_chan.get(chan, []) + gate_all)
                    g_chan[chan] = []
                ev = s.dv(lambda e, oc=oc, tt=tt, pp=pp, slot=slot: e.tensor_tensor(
                    arv(oc)[:, tt * 512:(tt + 1) * 512],
                    xsc2[:, slot * T + tt * 512: slot * T + (tt + 1) * 512], pp, ALU.add),
                    deps=[mml, xdma[oc], yg_ops[(oc, tt)]])
                res1_ops[(oc, tt)] = ev
                g_chan[chan].append(ev)
                return [ev]

            proj_fm(dw["woT"], 0, ygv, lambda ci, tt: [yg_ops[(ci, tt)]],
                    ev_wo, ps_proj, nreg=4, pre_marks=marks_wo, pre_dg2=2)
        st_es.close()

        def x1_ready(cc):
            return [res1_ops[(cc, tt)] for tt in range(4)]

        def ffn_layer(resid_in_ops):
            marks_f = s.mark()
            f_es = contextlib.ExitStack()
            with f_es:
                bcs = f_es.enter_context(nc.sbuf_tensor(_nm("bcs_f"), [128, T], F32))
                bch = f_es.enter_context(nc.sbuf_tensor(_nm("bch_f"), [128, T], F32))
                src = mk_sbuf_src(arv, lambda cc: resid_in_ops(cc))
                applyF, applyF_tt, _ = emit_ln(
                    src, bcs, bch, cast_dst_fn=xnv, apply_dst_fn=xnv,
                    extra_src_deps_fn=lambda half: [
                        op for c in range(CH) for op in resid_in_ops(c)])
            m_es = contextlib.ExitStack()
            xr = m_es.enter_context(nc.sbuf_tensor(_nm("xr_f"), [128, CH * T], BF16))
            xk = m_es.enter_context(nc.sbuf_tensor(_nm("xk_f"), [128, CH * T], BF16))
            d_s = m_es.enter_context(nc.sbuf_tensor(_nm("d_f"), [128, 3072], BF16))

            def xrv(cc):
                return xr[:, cc * T:(cc + 1) * T]

            def xkv(cc):
                return xk[:, cc * T:(cc + 1) * T]

            mix_r = {}
            mix_k = {}
            mix_all = {}
            d_war = {i: [] for i in range(4)}
            for g in (1, 2):
                for i in range(4):
                    d_war[(g, i)] = []
            for cc in range(CH):
                for tt in range(4):
                    t0 = tt * 512
                    hi = t0 + 512
                    adep = [applyF_tt[(cc, tt)]]
                    if tt > 0:
                        adep.append(applyF_tt[(cc, tt - 1)])
                    sl = (cc + tt) % 2
                    dsv = d_s[:, sl * 512:sl * 512 + 512]
                    dops = []
                    if tt == 0:
                        dops.append(s.gp(lambda e, cc=cc, dsv=dsv: e.tensor_copy(
                            dsv[:, 0:1], xnv(cc)[:, 0:1]), deps=adep + d_war[sl]))
                        dops.append(s.gp(lambda e, cc=cc, dsv=dsv: e.tensor_tensor(
                            dsv[:, 1:512], xnv(cc)[:, 1:512], xnv(cc)[:, 0:511],
                            ALU.subtract), deps=adep + d_war[sl]))
                    else:
                        dops.append(s.gp(lambda e, cc=cc, dsv=dsv, t0=t0, hi=hi:
                                         e.tensor_tensor(
                            dsv[:], xnv(cc)[:, t0:hi], xnv(cc)[:, t0 - 1:hi - 1],
                            ALU.subtract), deps=adep + d_war[sl]))
                    mrv = d_s[:, 1024 + sl * 512:1024 + sl * 512 + 512]
                    mkv = d_s[:, 2048 + sl * 512:2048 + sl * 512 + 512]
                    mr = s.dv(lambda e, dsv=dsv, mrv=mrv, cc=cc: e.tensor_scalar_mul(
                        mrv, dsv[:], murc[:, cc:cc + 1]),
                        deps=dops + [ld["murc"]] + d_war[(1, sl)])
                    mk = s.dv(lambda e, dsv=dsv, mkv=mkv, cc=cc: e.tensor_scalar_mul(
                        mkv, dsv[:], mukc[:, cc:cc + 1]),
                        deps=dops + [ld["mukc"]] + d_war[(2, sl)])
                    r1 = s.dv(lambda e, cc=cc, mrv=mrv, t0=t0, hi=hi: e.tensor_tensor(
                        xrv(cc)[:, t0:hi], mrv, xnv(cc)[:, t0:hi], ALU.add),
                        deps=[mr])
                    k1 = s.dv(lambda e, cc=cc, mkv=mkv, t0=t0, hi=hi: e.tensor_tensor(
                        xkv(cc)[:, t0:hi], mkv, xnv(cc)[:, t0:hi], ALU.add),
                        deps=[mk])
                    mix_r[(cc, tt)] = [r1]
                    mix_k[(cc, tt)] = [k1]
                    mix_all[(cc, tt)] = dops + [r1, k1]
                    d_war[sl] = [mr, mk]
                    d_war[(1, sl)] = [r1]
                    d_war[(2, sl)] = [k1]

            kact_ops = {}
            k_scratch = []
            with contextlib.ExitStack() as les:
                ps_proj = les.enter_context(nc.psum_tensor(_nm("ps_k"), [128, 4096], F32))

                def ev_k(oc, tt, pp, mml, par):
                    sB = p_scr[:, par * 1024:par * 1024 + 512]
                    t1 = s.dv(lambda e, pp=pp, sB=sB: e.tensor_scalar_max(sB, pp, 0.0),
                              deps=[mml] + kact_ops.get(("free", par), []))
                    t2 = s.ac(lambda e, oc=oc, tt=tt, sB=sB: e.activation(
                        xnv(oc)[:, tt * 512:(tt + 1) * 512], sB, AF.Square),
                        deps=[t1] + mix_all[(oc, tt)]
                        + (mix_all[(oc, tt + 1)] if tt < 3 else []))
                    kact_ops[(oc, tt)] = t2
                    kact_ops[("free", par)] = [t2]
                    k_scratch.append(t2)
                    return [t1, t2]

                proj_fm(dw["wkT"], 0, xkv, lambda ci, tt: mix_k[(ci, tt)],
                        ev_k, ps_proj, nreg=4, pre_marks=marks_f, pre_dg2=2)

            res_ops = {}
            with contextlib.ExitStack() as les:
                ps_r = les.enter_context(nc.psum_tensor(_nm("ps_r"), [128, 2048], F32))
                ps_v = les.enter_context(nc.psum_tensor(_nm("ps_v2"), [128, 2048], F32))
                t3_scr = p_scr[:, 3072:4096]
                sig_ops = {}

                def ev_r(oc, tt, pp, mml, par):
                    dd = oc % 2
                    sB = p_scr[:, (par * 2 + dd) * 512:(par * 2 + dd) * 512 + 512]
                    sg = s.ac(lambda e, pp=pp, sB=sB: e.activation(sB, pp, AF.Sigmoid),
                              deps=[mml] + sig_ops.get(("free", par, dd), k_scratch))
                    sig_ops[(oc, tt)] = (sg, sB)
                    return [sg]

                def ev_v2(oc, tt, pp, mml, par):
                    dd = oc % 2
                    sg, sB = sig_ops[(oc, tt)]
                    t3 = t3_scr[:, dd * 512:(dd + 1) * 512]
                    m1 = s.dv(lambda e, pp=pp, sB=sB, t3=t3: e.tensor_tensor(
                        t3, sB, pp, ALU.mult), deps=[mml, sg] + res_ops.get(("free", dd), []))
                    sig_ops[("free", par, dd)] = [m1]
                    rr = s.gp(lambda e, oc=oc, tt=tt, t3=t3: e.tensor_tensor(
                        arv(oc)[:, tt * 512:(tt + 1) * 512],
                        arv(oc)[:, tt * 512:(tt + 1) * 512], t3, ALU.add),
                        deps=[m1] + resid_in_ops(oc))
                    res_ops[(oc, tt)] = rr
                    res_ops[("free", dd)] = [rr]
                    return [m1]

                proj_fm(dw["wrT"], 0, xrv, lambda ci, tt: mix_r[(ci, tt)],
                        ev_r, ps_r,
                        second=(dw["wfvT"], 0, xnv,
                                lambda ci, tt: [kact_ops[(ci, tt)]],
                                ev_v2, ps_v))
            m_es.close()
            return lambda cc: [res_ops[(cc, tt)] for tt in range(4)]

        x2_ready = ffn_layer(x1_ready) if stop_after >= 2 else x1_ready

        if stop_after >= 3:
            b_es = contextlib.ExitStack()
            bcs_b = b_es.enter_context(nc.sbuf_tensor(_nm("bcs_b"), [128, T], F32))
            bch_b = b_es.enter_context(nc.sbuf_tensor(_nm("bch_b"), [128, T], F32))
            marks_b = s.mark()
            src_b = mk_sbuf_src(arv, lambda cc: x2_ready(cc))
            all_res_b = [op for c in range(CH) for op in x2_ready(c)]
            applyB, applyB_tt, _ = emit_ln(src_b, bcs_b, bch_b, cast_dst_fn=xnv,
                                           apply_dst_fn=xnv,
                                           extra_src_deps_fn=lambda half: all_res_b,
                                           apply_gp_cut=4)
            b_es.close()
            spill = {}
            for cc in range(CH):
                spill[cc] = s.dma_gp(lambda e, cc=cc: e.dma_start(
                    out=d_xs1[cc * 128:(cc + 1) * 128, :], in_=arv(cc)),
                    f"spill{cc}", deps=x2_ready(cc) + applyB[cc])

            def qv(cc):
                return arena_bf[:, cc * T:(cc + 1) * T]

            def kv(cc):
                return arena_bf[:, (CH + cc) * T:(CH + cc + 1) * T]

            q_ops = {}
            k_ops = {}
            qk_es = contextlib.ExitStack()
            v_tok = qk_es.enter_context(nc.sbuf_tensor(_nm("v_tok"), [128, 16 * 1024], BF16))
            vs_es = contextlib.ExitStack()
            v_scr = vs_es.enter_context(nc.sbuf_tensor(_nm("v_scr"), [128, 2 * T], BF16))
            vt_r = v_tok.ap().rearrange("p (j c) -> p j c", c=1024)
            vtok_ready = {}
            with contextlib.ExitStack() as les:
                ps_proj = les.enter_context(nc.psum_tensor(_nm("ps_q"), [128, 2048], F32))

                def ev_qb(oc, tt, pp, mml, par):
                    ev = s.ac(lambda e, oc=oc, tt=tt, pp=pp: e.activation(
                        qv(oc)[:, tt * 512:(tt + 1) * 512], pp, AF.Identity, scale=QSCALE),
                        deps=[mml, spill[oc // 2]])
                    q_ops[(oc, tt)] = ev
                    return [ev]

                proj_fm(d_wqkv, 0, xnv, lambda ci, tt: [applyB_tt[(ci, tt)]], ev_qb, ps_proj,
                        pre_marks=marks_b, pre_dg2=2)

                def ev_kb(oc, tt, pp, mml, par):
                    ev = s.ac(lambda e, oc=oc, tt=tt, pp=pp: e.activation(
                        kv(oc)[:, tt * 512:(tt + 1) * 512], pp, AF.Identity),
                        deps=[mml, spill[4 + oc // 2]])
                    k_ops[(oc, tt)] = ev
                    return [ev]

                proj_fm(d_wqkv, D, xnv, lambda ci, tt: [applyB_tt[(ci, tt)]], ev_kb, ps_proj)

                ps_tr = les.enter_context(nc.psum_tensor(_nm("ps_tr"), [128, 2048], BF16))
                vscr_ops = {}
                vscr_war = {0: [], 1: []}
                tr_war = {i: list(gate["ps"]) for i in range(8)}

                def ev_vb(oc, tt, pp, mml, par):
                    slot = oc % 2
                    ev = s.ac(lambda e, oc=oc, tt=tt, pp=pp, slot=slot: e.activation(
                        v_scr[:, slot * T + tt * 512: slot * T + (tt + 1) * 512], pp,
                        AF.Identity), deps=[mml] + vscr_war[slot])
                    vscr_ops.setdefault(oc, []).append(ev)
                    ret = [ev]
                    if tt == 3:
                        war_new = []
                        for g in range(4):
                            trs = []
                            for j in range(4):
                                j0 = g * 4 + j
                                pslot = (j0 % 8) * 256
                                tr = s.pe(lambda e, oc=oc, slot=slot, j0=j0, pslot=pslot: e.transpose(
                                    ps_tr[:, pslot:pslot + 128],
                                    v_scr[:, slot * T + j0 * 128: slot * T + (j0 + 1) * 128],
                                    idn[:]),
                                    deps=vscr_ops[oc] + [ld["idn"]] + tr_war.get(j0 % 8, []))
                                trs.append(tr)
                            base = (g * 4) % 8
                            cp = s.dv(lambda e, oc=oc, g=g, base=base: e.tensor_copy(
                                vt_r[:, g * 4:(g + 1) * 4, oc * 128:(oc + 1) * 128],
                                ps_tr[:, base * 256: base * 256 + 1024]
                                .rearrange("p (j c) -> p j c", c=256)[:, :, 0:128]), deps=trs)
                            for j in range(4):
                                tr_war[(g * 4 + j) % 8] = [cp]
                            vtok_ready[(oc, g)] = cp
                            war_new.append(cp)
                        vscr_war[slot] = war_new
                    return ret

                proj_fm(d_wqkv, 2 * D, xnv, lambda ci, tt: [applyB_tt[(ci, tt)]], ev_vb, ps_proj)

            vs_es.close()

            n_es = contextlib.ExitStack()
            negM = n_es.enter_context(nc.sbuf_tensor(_nm("negM"), [128, 16], F32))
            mrow = n_es.enter_context(nc.sbuf_tensor(_nm("mrow"), [16, 4], F32))
            mrow_bf = n_es.enter_context(nc.sbuf_tensor(_nm("mrow_bf"), [16, 1], BF16))
            nrow_bf = n_es.enter_context(nc.sbuf_tensor(_nm("nrow_bf"), [1, 16], BF16))
            with contextlib.ExitStack() as les:
                ps_m = les.enter_context(nc.psum_tensor(_nm("ps_m"), [16, 2048], F32))
                ps_n1 = les.enter_context(nc.psum_tensor(_nm("ps_n1"), [1, 1024], BF16))
                ps_n2 = les.enter_context(nc.psum_tensor(_nm("ps_n2"), [128, 512], F32))
                red_ops = []
                for qk in range(2):
                    view = qv if qk == 0 else kv
                    vops = q_ops if qk == 0 else k_ops
                    mm_all = []
                    sq_prev = []
                    for cc in range(CH):
                        sqo = s.ac(lambda e, cc=cc, view=view: e.activation(
                            sq_scr[:, 0:T], view(cc), AF.Square),
                            deps=[vops[(cc, tt)] for tt in range(4)] + sq_prev)
                        mms = []
                        for ip in range(4):
                            mms.append(s.pe(lambda e, cc=cc, ip=ip: e.matmul(
                                ps_m[:, ip * 512:(ip + 1) * 512],
                                Gind[:, 14 - 2 * cc:30 - 2 * cc],
                                sq_scr[:, ip * 512:(ip + 1) * 512],
                                start=(cc == 0), stop=(cc == CH - 1)),
                                deps=[sqo, ld["Gind"]] + red_ops
                                + (gate["ps"] if cc == 0 else [])))
                        sq_prev = mms
                        mm_all = mms
                    red = s.dv(lambda e, qk=qk: e.reduce_max(
                        mrow[:, qk:qk + 1], ps_m[:, 0:2048], axis=AX.X), deps=mm_all)
                    sq = s.ac(lambda e, qk=qk: e.activation(
                        mrow[:, 2 + qk:3 + qk], mrow[:, qk:qk + 1], AF.Sqrt), deps=[red])
                    red_ops = [red, sq]
                nm = s.dv(lambda e: e.tensor_scalar(
                    mrow[:, 0:1], mrow[:, 2:3], mrow[:, 3:4], -1.0, ALU.mult, ALU.mult),
                    deps=red_ops)
                nmb = s.dv(lambda e: e.tensor_copy(mrow_bf[:], mrow[:, 0:1]), deps=[nm])
                trn = s.pe(lambda e: e.transpose(ps_n1[0:1, 0:16], mrow_bf[:],
                                                 idn[0:16, 0:16]),
                           deps=[nmb, ld["idn"]] + list(vtok_ready.values()))
                nrw = s.dv(lambda e: e.tensor_copy(nrow_bf[:], ps_n1[0:1, 0:16]), deps=[trn])
                bco = []
                for h in range(16):
                    bm = s.pe(lambda e, h=h: e.matmul(
                        ps_n2[:, h:h + 1], ones_rb[:], nrow_bf[0:1, h:h + 1],
                        start=True, stop=True), deps=[nrw, c_rb])
                    bco.append(bm)
                negM_op = s.dv(lambda e: e.tensor_copy(negM[:], ps_n2[:, 0:16]), deps=bco)
                gate["ps"] = [negM_op, nrw] + red_ops

            at_es = contextlib.ExitStack()
            Dbuf = at_es.enter_context(nc.sbuf_tensor(_nm("Dbuf"), [128, 4 * T], BF16))
            Ebuf = at_es.enter_context(nc.sbuf_tensor(_nm("Ebuf"), [128, 4096], BF16))
            Ebuf2 = at_es.enter_context(nc.sbuf_tensor(_nm("Ebuf2"), [128, 4096], BF16))
            vhat = at_es.enter_context(nc.sbuf_tensor(_nm("vhat"), [128, 32 * 66], BF16))
            vhat_i32 = vhat.ap().bitcast(mybir.dt.int32).rearrange(
                "p (j c) -> p j c", c=33)
            vt_i32 = v_tok.ap().bitcast(mybir.dt.int32).rearrange(
                "p (j c) -> p j c", c=512)
            rsum = [[ln_scr[64 * ln:64 * ln + 1, r * 1024:(r + 1) * 1024]
                     for r in range(2)] for ln in range(2)]
            rlbf = [[sq_scr[0:1, ln * 1024:(ln + 1) * 1024]
                     for r in range(2)] for ln in range(2)]
            rlbc = [p_scr[ln * 64:(ln + 1) * 64, 2048:3072] for ln in range(2)]
            vtr = list(vtok_ready.values())
            ld_D = {}
            d_war = {0: [], 1: []}

            def dview(cc, lane):
                return Dbuf[:, ((cc % 2) * 2 + lane) * T:((cc % 2) * 2 + lane) * T + T]
            vh_ms = s.dv(lambda e: e.memset(vhat[:], 1.0))
            out_ready = {}
            marks_w = s.mark()
            with contextlib.ExitStack() as les:
                psS = [les.enter_context(nc.psum_tensor(_nm("ps_Sa"), [128, 1024], F32)),
                       les.enter_context(nc.psum_tensor(_nm("ps_Sb"), [128, 1024], F32))]
                psU = [les.enter_context(nc.psum_tensor(_nm("ps_Ua"), [65, 1024], F32)),
                       les.enter_context(nc.psum_tensor(_nm("ps_Ub"), [65, 1024], F32))]
                S_war = {ln: list(gate["ps"]) for ln in range(2)}
                E_war = {(ln, sl): [] for ln in range(2) for sl in range(2)}
                E2_war = {(ln, sl): [] for ln in range(2) for sl in range(2)}
                stt_war = {(ln, sl): [] for ln in range(2) for sl in range(2)}
                U_war = {0: list(gate["ps"]), 1: list(gate["ps"])}
                vh_war = {0: [vh_ms], 1: [vh_ms]}
                vcp = {}
                rsum_war = {(ln, r): [] for ln in range(2) for r in range(2)}
                rlbf_war = {ln: [] for ln in range(2)}
                rlbc_war = {0: [], 1: []}
                defer = {0: None, 1: None}
                it_idx = [0]

                def emit_deferred(lane):
                    if defer[lane] is not None:
                        defer[lane]()
                        defer[lane] = None

                def mk_defer(cc, lane, ihalf, smc, ev0, ring):
                    def emit():
                        rs_v = rsum[lane][ring]
                        rb_v = rlbf[lane][ring]
                        ln0 = s.ac(lambda e, rs_v=rs_v: e.activation(
                            rs_v, rs_v, AF.Ln), deps=[smc])
                        rex = s.ac(lambda e, rs_v=rs_v: e.activation(
                            rs_v, rs_v, AF.Exp, scale=-1.0), deps=[ln0])
                        rlb = s.dv(lambda e, rs_v=rs_v, rb_v=rb_v: e.tensor_copy(
                            rb_v, rs_v), deps=[rex] + rlbf_war[lane])
                        rsum_war[(lane, ring)] = [rlb]
                        bms = []
                        for ip in range(2):
                            bm = s.pe(lambda e, lane=lane, ip=ip, rb_v=rb_v: e.matmul(
                                psU[lane][0:64, ip * 512:(ip + 1) * 512],
                                ones_rb[0:1, 0:64], rb_v[0:1, ip * 512:(ip + 1) * 512],
                                start=True, stop=True),
                                deps=[rlb, c_rb, ev0, smc] + rlbc_war[lane])
                            bms.append(bm)
                        rlbf_war[lane] = bms
                        rbe = s.dv(lambda e, lane=lane: e.tensor_copy(
                            rlbc[lane][:], psU[lane][0:64, 0:1024]), deps=bms)
                        U_war[lane] = U_war[lane] + [rbe]
                        i0 = ihalf * 1024
                        mul = s.dv(lambda e, cc=cc, lane=lane, i0=i0: e.tensor_tensor(
                            qv(cc)[lane * 64:(lane + 1) * 64, i0:i0 + 1024],
                            qv(cc)[lane * 64:(lane + 1) * 64, i0:i0 + 1024],
                            rlbc[lane][:], ALU.mult), deps=[rbe, ev0])
                        rlbc_war[lane] = [mul]
                        out_ready[(cc, lane, ihalf)] = mul
                    return emit

                for cc in range(CH):
                    if cc not in ld_D:
                        slot = cc % 2
                        dl_ = []
                        for lane in range(2):
                            h = 2 * cc + lane
                            dl_.append(s.dma_gp(lambda e, h=h, lane=lane, slot=slot: e.dma_start(
                                out=Dbuf[:, (slot * 2 + lane) * T:(slot * 2 + lane) * T + T],
                                in_=d_Dm[h * 128:(h + 1) * 128, :]),
                                f"dm{slot}{lane}", deps=d_war[slot] + (vtr if cc < 2 else [])))
                        ld_D[cc] = dl_
                    for lane in range(2):
                        h = 2 * cc + lane
                        vcp[(cc, lane)] = s.dv(lambda e, h=h, lane=lane: e.tensor_copy(
                            vhat_i32[:, lane * 16:(lane + 1) * 16, 0:32],
                            vt_i32[:, :, h * 32:h * 32 + 32]),
                            deps=[vtok_ready[(cc, g)] for g in range(4)] + vh_war[lane])
                        vh_war[lane] = []
                    for ihalf in range(2):
                        i0 = ihalf * 1024
                        nj = 8 if ihalf == 0 else 16
                        pend = {0: None, 1: None}
                        pv_ops = {0: [], 1: []}
                        for j0 in range(nj):
                            jb = 128 * j0
                            lo0 = max(jb, i0)
                            wtot = i0 + 1024 - lo0
                            sl = j0 % 2
                            smm = {}
                            for lane in range(2):
                                mlist = []
                                for kk in range(2):
                                    blo = max(lo0, i0 + kk * 512)
                                    bhi = i0 + (kk + 1) * 512
                                    if blo >= bhi:
                                        continue
                                    mlist.append(s.pe(lambda e, cc=cc, lane=lane, jb=jb,
                                                      blo=blo, bhi=bhi, i0=i0: e.matmul(
                                        psS[lane][:, blo - i0:bhi - i0],
                                        kv(cc)[lane * 64:(lane + 1) * 64, jb:jb + 128],
                                        qv(cc)[lane * 64:(lane + 1) * 64, blo:bhi],
                                        start=True, stop=True, tile_position=(lane * 64, 0)),
                                        deps=[k_ops[(cc, jb // 512)], q_ops[(cc, blo // 512)],
                                              q_ops[(cc, (bhi - 1) // 512)]] + S_war[lane]))
                                smm[lane] = mlist
                            if j0 == 1:
                                emit_deferred(0)
                                emit_deferred(1)
                            for lane in range(2):
                                if pend[lane] is not None:
                                    pend[lane]()
                                    pend[lane] = None
                            for lane in range(2):
                                h = 2 * cc + lane
                                ebase = (lane * 2 + sl) * 1024
                                ex = s.ac(lambda e, ebase=ebase, wtot=wtot, h=h, lane=lane,
                                          lo0=lo0, i0=i0: e.activation(
                                    Ebuf[:, ebase:ebase + wtot],
                                    psS[lane][:, lo0 - i0:1024],
                                    AF.Exp, bias=negM[:, h:h + 1]),
                                    deps=smm[lane] + [negM_op] + E_war[(lane, sl)])
                                S_war[lane] = [ex]
                                dm = s.dv(lambda e, ebase=ebase, wtot=wtot, cc=cc, lane=lane,
                                          lo0=lo0, jb=jb: e.tensor_tensor(
                                    Ebuf2[:, ebase:ebase + wtot],
                                    Ebuf[:, ebase:ebase + wtot],
                                    dview(cc, lane)[:, lo0 - jb:lo0 - jb + wtot], ALU.mult),
                                    deps=[ex] + ld_D[cc] + E2_war[(lane, sl)])
                                stt_war[(lane, sl)] = [dm]
                                E_war[(lane, sl)] = [dm]

                                def mk_pv(lane=lane, j0=j0, sl=sl, lo0=lo0, ex=dm,
                                          asmop=vcp[(cc, lane)], i0=i0, nj=nj,
                                          ebase=ebase, cc=cc):
                                    def emit():
                                        outp = []
                                        for kk in range(2):
                                            blo = max(lo0, i0 + kk * 512)
                                            bhi = i0 + (kk + 1) * 512
                                            if blo >= bhi:
                                                continue
                                            lastj = min(nj - 1, (bhi - 1) // 128)
                                            pv = s.pe(lambda e, lane=lane, blo=blo, bhi=bhi,
                                                      i0=i0, sl=sl, j0=j0, lastj=lastj,
                                                      lo0=lo0, ebase=ebase: e.matmul(
                                                psU[lane][:, blo - i0:bhi - i0],
                                                vhat[:, (lane * 16 + j0) * 66:
                                                     (lane * 16 + j0) * 66 + 65],
                                                Ebuf2[:, ebase + blo - lo0:ebase + bhi - lo0],
                                                start=(j0 == 0), stop=(j0 == lastj)),
                                                deps=[ex, asmop] + (U_war[lane] if j0 == 0 else []))
                                            outp.append(pv)
                                        pv_ops[lane] += outp
                                        E2_war[(lane, sl)] = outp
                                        vh_war[lane] = vh_war[lane] + outp
                                    return emit
                                pend[lane] = mk_pv()
                        for lane in range(2):
                            if pend[lane] is not None:
                                pend[lane]()
                                pend[lane] = None
                        if ihalf == 1:
                            d_war[cc % 2] = [op for lst in (stt_war[(0, 0)], stt_war[(0, 1)],
                                                            stt_war[(1, 0)], stt_war[(1, 1)]) for op in lst]
                        ring = it_idx[0] % 2
                        it_idx[0] += 1
                        for lane in range(2):
                            ev0 = s.dv(lambda e, cc=cc, lane=lane, i0=i0: e.tensor_copy(
                                qv(cc)[lane * 64:(lane + 1) * 64, i0:i0 + 1024],
                                psU[lane][0:64, 0:1024]),
                                deps=pv_ops[lane])
                            smc = s.dv(lambda e, lane=lane, ring=ring: e.tensor_copy(
                                rsum[lane][ring], psU[lane][64:65, 0:1024]),
                                deps=pv_ops[lane] + rsum_war[(lane, ring)])
                            U_war[lane] = [ev0, smc]
                            defer[lane] = mk_defer(cc, lane, ihalf, smc, ev0, ring)
                emit_deferred(0)
                emit_deferred(1)
                gate["ps"] = (U_war[0] + U_war[1] + S_war[0] + S_war[1]
                              + rlbc_war[0] + rlbc_war[1])
            at_es.close()
            n_es.close()

            res3_dma = {}

            def rhs_out_deps(ci, tt):
                return [out_ready[(ci, lane, tt // 2)] for lane in range(2)]

            with contextlib.ExitStack() as les:
                ps_proj = les.enter_context(nc.psum_tensor(_nm("ps_wo2"), [128, 4096], F32))
                x1sc = les.enter_context(nc.sbuf_tensor(_nm("x1sc"), [128, 2 * T], F32))
                x3sc = les.enter_context(nc.sbuf_tensor(_nm("x3sc"), [128, 2 * T], F32))
                xdma = {}
                stt3 = {}
                x3prev = {}

                def ev_wo2(oc, tt, pp, mml, par):
                    slot = oc % 2
                    chan = f"str{slot}"
                    chan3 = f"xst{slot}"
                    if oc not in xdma:
                        xdma[oc] = s.dma_gp(lambda e, oc=oc, slot=slot: e.dma_start(
                            out=x1sc[:, slot * T:slot * T + T],
                            in_=d_xs1[oc * 128:(oc + 1) * 128, :]), chan,
                            deps=g_chan.get(chan, []) + gate["ps"])
                        g_chan[chan] = []
                    ev = s.dv(lambda e, oc=oc, tt=tt, pp=pp, slot=slot: e.scalar_tensor_tensor(
                        x3sc[:, slot * T + tt * 512: slot * T + (tt + 1) * 512], pp,
                        boutc[:, oc:oc + 1],
                        x1sc[:, slot * T + tt * 512: slot * T + (tt + 1) * 512],
                        ALU.add, ALU.add),
                        deps=[mml, xdma[oc], ld["boutc"], x3prev.get(slot)]
                        + ([] if slot in x3prev else gate["ps"]))
                    g_chan[chan].append(ev)
                    stt3.setdefault(oc, []).append(ev)
                    if tt == 3:
                        dmo = s.dma_gp(lambda e, oc=oc, slot=slot: e.dma_start(
                            out=d_xs2[oc * 128:(oc + 1) * 128, :],
                            in_=x3sc[:, slot * T:slot * T + T]), chan3,
                            deps=stt3[oc] + g_chan.get(chan3, []))
                        g_chan[chan3] = []
                        x3prev[slot] = dmo
                        res3_dma[oc] = dmo
                    return [ev]

                proj_fm(dw["wout"], 0, qv, rhs_out_deps, ev_wo2, ps_proj, nreg=4,
                        pre_marks=marks_w, pre_dg2=2)
            qk_es.close()

            x3_ld = {}
            for cc in range(CH):
                x3_ld[cc] = s.dma_gp(lambda e, cc=cc: e.dma_start(
                    out=arv(cc), in_=d_xs2[cc * 128:(cc + 1) * 128, :]),
                    f"xld{cc}", deps=[res3_dma[cc]] + rhs_out_deps(cc, 0)
                    + rhs_out_deps(cc, 2))

            def x3_ready(cc):
                return [x3_ld[cc]]
        else:
            x3_ready = x2_ready

        x4_ready = ffn_layer(x3_ready) if stop_after >= 4 else x3_ready

        out_dmas = []
        for cc in range(CH):
            od = s.dma_gp(lambda e, cc=cc: e.dma_start(
                out=d_out[cc * 128:(cc + 1) * 128, :], in_=arv(cc)),
                "out", deps=x4_ready(cc), bulk=True)
            out_dmas.append(od)

        s.assign(sems, mk_dma_sem)
        s.final.append(("gp", out_dmas[-1]))
        with nc.Block() as block:
            @block.tensor
            def _(eng):
                s.emit("pe", eng)

            @block.vector
            def _(eng):
                s.emit("dv", eng)

            @block.scalar
            def _(eng):
                s.emit("ac", eng)

            @block.gpsimd
            def _(eng):
                s.emit("gp", eng)

            @block.sync
            def _(eng):
                s.emit("sq", eng)

    return nc


def _host_inputs(inputs):
    f32 = np.float32
    x = np.ascontiguousarray(np.transpose(inputs["L0_attn_a_x"], (0, 2, 1)))
    ins = {}
    ins["wqT"] = np.ascontiguousarray(inputs["L0_attn_a_Wq"].T)
    ins["wvT"] = np.ascontiguousarray(inputs["L0_attn_a_Wv"].T)
    ins["wgT"] = np.ascontiguousarray(inputs["L0_attn_a_Wg"].T)
    ins["woT"] = np.ascontiguousarray(inputs["L0_attn_a_Wo"].T)
    ins["wrT"] = np.ascontiguousarray(inputs["L0_ffn_W_r"].T)
    ins["wkT"] = np.ascontiguousarray(inputs["L0_ffn_W_k"].T)
    ins["wfvT"] = np.ascontiguousarray(inputs["L0_ffn_W_v"].T)
    ins["wout"] = np.ascontiguousarray(inputs["L0_attn_b_w_out"])
    ins["wqkv"] = np.ascontiguousarray(inputs["L0_attn_b_w_qkv"])

    def cols(v):
        return np.ascontiguousarray(np.asarray(v, dtype=f32).reshape(CH, 128).T)

    ins["bout_c"] = cols(inputs["L0_attn_b_b_out"])
    ins["mur_c"] = cols(np.asarray(inputs["L0_ffn_mu_r"], dtype=f32) - 1.0)
    ins["muk_c"] = cols(np.asarray(inputs["L0_ffn_mu_k"], dtype=f32) - 1.0)
    ins["gam_c"] = cols(inputs["L0_attn_a_ln_gamma"])
    ins["bet_c"] = cols(inputs["L0_attn_a_ln_beta"])
    beta = np.asarray(inputs["L0_attn_a_beta"], dtype=f32)
    p = np.arange(128)
    ccg = np.arange(CH)
    ins["beta_pc"] = np.ascontiguousarray(beta[(2 * ccg[None, :] + (p[:, None] >= 64))])
    pp = np.arange(128)[:, None]
    cc_ = np.arange(T)[None, :]
    rel = np.where(cc_ >= pp, -(cc_ - pp).astype(np.float64), NEGBIG).astype(f32)
    ins["relmaster"] = np.ascontiguousarray(rel)
    import ml_dtypes
    slopes = _slopes16()
    Dm = np.zeros((NH * 128, T), dtype=ml_dtypes.bfloat16)
    for h in range(NH):
        Dm[h * 128:(h + 1) * 128, :] = np.exp(slopes[h] * rel.astype(np.float64)).astype(ml_dtypes.bfloat16)
    ins["Dmaster"] = Dm
    G = np.zeros((128, 32), dtype=ml_dtypes.bfloat16)
    G[0:64, 14] = 1
    G[64:128, 15] = 1
    ins["Gind"] = G
    ins["idn"] = np.eye(128, dtype=ml_dtypes.bfloat16)
    return x, ins


def kernel(**inputs):
    x, ins = _host_inputs(inputs)
    nc = build(STOP_AFTER)
    in_maps = []
    for b in range(8):
        m = dict(ins)
        m["xT"] = np.ascontiguousarray(x[b])
        in_maps.append(m)
    res = run_bass_kernel_spmd(nc, in_maps, core_ids=list(range(8)))
    out = np.stack([np.asarray(res.results[b]["out"]) for b in range(8)], axis=0)
    return np.ascontiguousarray(out.transpose(0, 2, 1)).astype(np.float32)


# version 22
# speedup vs baseline: 1.1911x; 1.0086x over previous
import contextlib
import numpy as np
import concourse.bass as bass
import concourse.mybir as mybir
from concourse.bass_utils import run_bass_kernel_spmd

F32 = mybir.dt.float32
BF16 = mybir.dt.bfloat16
AF = mybir.ActivationFunctionType
ALU = mybir.AluOpType
AX = mybir.AxisListType

T = 2048
HF = 1024
D = 1024
CH = 8
NH = 16
LN_EPS = 1e-5
QSCALE = 0.125
NEGBIG = -1e30
STOP_AFTER = 4


def _slopes16():
    start = 2.0 ** (-(2.0 ** -1.0))
    return [start * start ** i for i in range(16)]


QUEUES = ("pe", "dv", "ac", "gp", "sq")


class Op:
    __slots__ = ("q", "fn", "deps", "dma", "inc", "tick", "sem", "chan", "bulk")

    def __init__(self, q, fn, deps, dma, chan=None, bulk=False):
        self.q = q
        self.fn = fn
        self.deps = [d for d in deps if d is not None]
        self.dma = dma
        self.inc = dma
        self.tick = None
        self.sem = None
        self.chan = chan
        self.bulk = bulk


class Sched:
    def __init__(self):
        self.seq = {q: [] for q in QUEUES}
        self.final = []
        self._ins = None

    def _op(self, q, fn, deps, dma=False, chan=None, bulk=False):
        o = Op(q, fn, deps, dma, chan, bulk)
        for d in o.deps:
            d.inc = True
        if self._ins is not None:
            self.seq[q].insert(self._ins[q], o)
            self._ins[q] += 1
        else:
            self.seq[q].append(o)
        return o

    def mark(self):
        return {q: len(self.seq[q]) for q in QUEUES}

    @contextlib.contextmanager
    def at_marks(self, marks):
        prev = self._ins
        self._ins = marks
        try:
            yield
        finally:
            self._ins = prev

    def pe(self, fn, deps=()):
        return self._op("pe", fn, deps)

    def dv(self, fn, deps=()):
        return self._op("dv", fn, deps)

    def ac(self, fn, deps=()):
        return self._op("ac", fn, deps)

    def gp(self, fn, deps=()):
        return self._op("gp", fn, deps)

    def dma_sq(self, fn, chan, deps=(), bulk=False):
        return self._op("sq", fn, deps, dma=True, chan=chan, bulk=bulk)

    def dma_gp(self, fn, chan, deps=(), bulk=False):
        return self._op("gp", fn, deps, dma=True, chan=chan, bulk=bulk)

    def assign(self, sems, mk_dma_sem):
        counters = {k: 0 for k in sems}
        dma_sems = {}
        dma_counters = {}
        bulk_ops = {}
        for q in QUEUES:
            for o in self.seq[q]:
                if not o.inc:
                    continue
                if o.dma:
                    if o.chan not in dma_sems:
                        dma_sems[o.chan] = mk_dma_sem(o.chan)
                        dma_counters[o.chan] = 0
                    dma_counters[o.chan] += 16
                    o.tick = dma_counters[o.chan]
                    o.sem = dma_sems[o.chan]
                    if o.bulk:
                        bulk_ops.setdefault(o.chan, []).append(o)
                else:
                    counters[o.q] += 1
                    o.tick = counters[o.q]
                    o.sem = sems[o.q]
        for chan, ops in bulk_ops.items():
            for o in ops:
                o.tick = dma_counters[chan]

    def emit(self, q, eng):
        hw = {}
        for o in self.seq[q]:
            for d in o.deps:
                key = id(d.sem)
                if hw.get(key, -1) < d.tick:
                    eng.wait_ge(d.sem, d.tick)
                    hw[key] = d.tick
            inst = o.fn(eng)
            if o.inc:
                inst.then_inc(o.sem, 16 if o.dma else 1)
        for (fq, o) in self.final:
            if fq == q:
                eng.wait_ge(o.sem, o.tick)


def build(stop_after=4, use_recip=False, dm_gp=True):
    nc = bass.Bass("TRN2", target_bir_lowering=False)
    s = Sched()
    slopes = _slopes16()
    _ncnt = [0]

    def _nm(b):
        _ncnt[0] += 1
        return f"{b}{_ncnt[0]}"

    d_x = nc.dram_tensor("xT", [D, T], F32, kind="ExternalInput").ap()
    dw = {}
    for nm in ("wqT", "wvT", "wgT", "woT", "wrT", "wkT", "wfvT", "wout"):
        dw[nm] = nc.dram_tensor(nm, [D, D], F32, kind="ExternalInput").ap()
    d_wqkv = nc.dram_tensor("wqkv", [D, 3 * D], F32, kind="ExternalInput").ap()
    d_bout = nc.dram_tensor("bout_c", [128, CH], F32, kind="ExternalInput").ap()
    d_mur = nc.dram_tensor("mur_c", [128, CH], F32, kind="ExternalInput").ap()
    d_muk = nc.dram_tensor("muk_c", [128, CH], F32, kind="ExternalInput").ap()
    d_gam = nc.dram_tensor("gam_c", [128, CH], F32, kind="ExternalInput").ap()
    d_bet = nc.dram_tensor("bet_c", [128, CH], F32, kind="ExternalInput").ap()
    d_betapc = nc.dram_tensor("beta_pc", [128, CH], F32, kind="ExternalInput").ap()
    d_rel = nc.dram_tensor("relmaster", [128, T], F32, kind="ExternalInput").ap()
    d_Dm = nc.dram_tensor("Dmaster", [NH * 128, T], BF16, kind="ExternalInput").ap()
    d_G = nc.dram_tensor("Gind", [128, 32], BF16, kind="ExternalInput").ap()
    d_idn = nc.dram_tensor("idn", [128, 128], BF16, kind="ExternalInput").ap()
    d_out = nc.dram_tensor("out", [D, T], F32, kind="ExternalOutput").ap()
    d_xs1 = nc.dram_tensor("xscr1", [D, T], F32, kind="Internal").ap()
    d_xs2 = nc.dram_tensor("xscr2", [D, T], F32, kind="Internal").ap()

    es = contextlib.ExitStack()
    with es:
        sems = {}
        for k in ("pe", "dv", "ac", "gp"):
            sems[k] = es.enter_context(nc.semaphore(k))

        def mk_dma_sem(chan):
            return es.enter_context(nc.semaphore(f"dma_{chan}"))

        xn = es.enter_context(nc.sbuf_tensor("xn", [128, CH * T], BF16))
        arena = es.enter_context(nc.sbuf_tensor("arena", [128, CH * T], F32))
        arena_bf = arena.ap().bitcast(BF16)
        sq_scr = es.enter_context(nc.sbuf_tensor("sq_scr", [128, T], BF16))
        w_stg = es.enter_context(nc.sbuf_tensor("w_stg", [128, 1024], F32))
        w_cache = es.enter_context(nc.sbuf_tensor("w_cache", [128, 4096], BF16))
        ln_scr = es.enter_context(nc.sbuf_tensor("ln_scr", [128, 2048], F32))
        p_scr = es.enter_context(nc.sbuf_tensor("p_scr", [128, 4096], F32))
        o128 = es.enter_context(nc.sbuf_tensor("o128", [128, 128], BF16))
        ones_rb = es.enter_context(nc.sbuf_tensor("ones_rb", [1, 128], BF16))
        lam_c = es.enter_context(nc.sbuf_tensor("lam_c", [128, CH], F32))
        betap = es.enter_context(nc.sbuf_tensor("betap", [128, CH], F32))
        murc = es.enter_context(nc.sbuf_tensor("murc", [128, CH], F32))
        mukc = es.enter_context(nc.sbuf_tensor("mukc", [128, CH], F32))
        gamc = es.enter_context(nc.sbuf_tensor("gamc", [128, CH], F32))
        betc = es.enter_context(nc.sbuf_tensor("betc", [128, CH], F32))
        boutc = es.enter_context(nc.sbuf_tensor("boutc", [128, CH], F32))
        Gind = es.enter_context(nc.sbuf_tensor("Gind_sb", [128, 32], BF16))
        idn = es.enter_context(nc.sbuf_tensor("idn_sb", [128, 128], BF16))

        def xnv(cc):
            return xn[:, cc * T:(cc + 1) * T]

        def arv(cc):
            return arena[:, cc * T:(cc + 1) * T]

        c_o128 = s.dv(lambda e: e.memset(o128[:], 1.0))
        c_rb = s.dv(lambda e: e.memset(ones_rb[:], 1.0))
        ld = {}
        for key, dst, src in (("betap", betap, d_betapc), ("murc", murc, d_mur),
                              ("mukc", mukc, d_muk), ("gamc", gamc, d_gam),
                              ("betc", betc, d_bet), ("boutc", boutc, d_bout),
                              ("Gind", Gind, d_G), ("idn", idn, d_idn)):
            ld[key] = s.dma_gp(lambda e, dst=dst, src=src: e.dma_start(out=dst[:], in_=src),
                               "const", bulk=True)
        lam_op = s.ac(lambda e: e.activation(lam_c[:], betap[:], AF.Sigmoid), deps=[ld["betap"]])

        g_stg = {0: [], 1: [], 2: [], 3: []}
        g_cache = {0: [], 1: []}
        gate = {"ps": []}
        g_chan = {}

        def mk_dram_src(dram_ap, buf):
            state = {"cnt": 0}

            def src(cc, half):
                slot = state["cnt"] % 4
                state["cnt"] += 1
                chan = f"xa{slot}"
                emit = s.dma_sq if slot < 2 else s.dma_gp
                d = emit(lambda e, cc=cc, half=half, slot=slot: e.dma_start(
                    out=buf[:, slot * HF:slot * HF + HF],
                    in_=dram_ap[cc * 128:(cc + 1) * 128, half * HF:half * HF + HF]),
                    chan, deps=g_chan.get(chan, []))
                marks = []
                g_chan[chan] = marks
                return buf[:, slot * HF:slot * HF + HF], [d], marks
            return src

        def mk_sbuf_src(view_fn, deps_fn):
            def src(cc, half):
                return view_fn(cc)[:, half * HF:half * HF + HF], list(deps_fn(cc)), []
            return src

        def emit_ln(src_fn, bcs, bch, cast_dst_fn, apply_dst_fn=None,
                    apply_from_cast=False, extra_src_deps_fn=None,
                    apply_gp_cut=6):
            apply_ops = [[] for _ in range(CH)]
            apply_tt = {}
            bc_done = []
            csts = {}
            with contextlib.ExitStack() as les:
                ps_sum = les.enter_context(nc.psum_tensor(_nm("ps_su"), [128, HF], F32))
                ps_sq = les.enter_context(nc.psum_tensor(_nm("ps_sq"), [128, HF], F32))
                stats_war = list(gate["ps"])
                sq_war = {0: [], 1: []}
                apply_war = {i: [] for i in range(4)}
                for half in range(2):
                    h0 = half * HF
                    mm_sum_last = []
                    mm_sq_last = []
                    for cc in range(CH):
                        src, sdeps, marks = src_fn(cc, half)
                        if extra_src_deps_fn is not None:
                            sdeps = sdeps + extra_src_deps_fn(half)
                        cdst = cast_dst_fn(cc)
                        so = (cc % 2) * HF
                        if cc % 2 == 0:
                            cst = s.ac(lambda e, src=src, cdst=cdst, h0=h0: e.activation(
                                cdst[:, h0:h0 + HF], src, AF.Identity), deps=sdeps)
                            sqc = s.dv(lambda e, src=src, so=so: e.tensor_tensor(
                                sq_scr[:, so:so + HF], src, src, ALU.mult),
                                deps=sdeps + sq_war[cc % 2])
                        else:
                            cst = s.dv(lambda e, src=src, cdst=cdst, h0=h0: e.tensor_copy(
                                cdst[:, h0:h0 + HF], src), deps=sdeps)
                            sqc = s.ac(lambda e, src=src, so=so: e.activation(
                                sq_scr[:, so:so + HF], src, AF.Square),
                                deps=sdeps + sq_war[cc % 2])
                        marks += [cst, sqc]
                        csts[(cc, half)] = cst
                        mm_q = []
                        mm_s = []
                        for ip in range(2):
                            mm_s.append(s.pe(lambda e, cdst=cdst, h0=h0, ip=ip, cc=cc: e.matmul(
                                ps_sum[:, ip * 512:(ip + 1) * 512], o128[:],
                                cdst[:, h0 + ip * 512:h0 + (ip + 1) * 512],
                                start=(cc == 0), stop=(cc == CH - 1)),
                                deps=[cst, c_o128] + (stats_war if cc == 0 else [])))
                            mm_q.append(s.pe(lambda e, so=so, ip=ip, cc=cc: e.matmul(
                                ps_sq[:, ip * 512:(ip + 1) * 512], o128[:],
                                sq_scr[:, so + ip * 512:so + (ip + 1) * 512],
                                start=(cc == 0), stop=(cc == CH - 1)),
                                deps=[sqc, c_o128] + (stats_war if cc == 0 else [])))
                        sq_war[cc % 2] = mm_q
                        mm_sum_last = mm_s
                        mm_sq_last = mm_q
                    rs = bcs[:, h0:h0 + HF]
                    rh = bch[:, h0:h0 + HF]
                    nm_v = ln_scr[:, h0:h0 + HF]
                    o1 = s.dv(lambda e, nm_v=nm_v: e.tensor_scalar_mul(
                        nm_v, ps_sum[:, 0:HF], -1.0 / D), deps=mm_sum_last)
                    o2 = s.dv(lambda e, rh=rh: e.tensor_scalar(
                        rh, ps_sq[:, 0:HF], 1.0 / D, LN_EPS, ALU.mult, ALU.add),
                        deps=mm_sq_last)
                    o3 = s.dv(lambda e, nm_v=nm_v: e.tensor_tensor(
                        ps_sum[:, 0:HF], nm_v, nm_v, ALU.mult), deps=[o1])
                    o4 = s.dv(lambda e, rh=rh: e.tensor_tensor(
                        rh, rh, ps_sum[:, 0:HF], ALU.subtract), deps=[o2, o3])
                    o5 = s.ac(lambda e, rh=rh: e.activation(
                        ps_sq[:, 0:HF], rh, AF.Ln), deps=[o4])
                    o6 = s.ac(lambda e, rs=rs: e.activation(
                        rs, ps_sq[:, 0:HF], AF.Exp, scale=-0.5), deps=[o5])
                    o7 = s.dv(lambda e, rh=rh, nm_v=nm_v, rs=rs: e.tensor_tensor(
                        rh, nm_v, rs, ALU.mult), deps=[o6, o4])
                    stats_war = [o4, o6]
                    bc_done += [o6, o7]
                    if apply_dst_fn is not None:
                        for cc in range(CH):
                            if apply_from_cast:
                                aview0 = cast_dst_fn(cc)[:, h0:h0 + HF]
                                adeps = [csts[(cc, half)]]
                            else:
                                aview0, adeps, _m = src_fn(cc, half)
                            for q in range(2):
                                q0 = q * 512
                                aview = aview0[:, q0:q0 + 512]
                                eng = s.dv if cc < apply_gp_cut else s.gp
                                slot = (cc * 2 + q) % 4
                                sB = p_scr[:, slot * 512:slot * 512 + 512]
                                t1 = eng(lambda e, aview=aview, sB=sB, rs=rs, q0=q0:
                                         e.tensor_tensor(
                                    sB, aview, rs[:, q0:q0 + 512], ALU.mult),
                                    deps=adeps + [o6] + apply_war[slot])
                                t2 = eng(lambda e, cc=cc, sB=sB, rh=rh, h0=h0, q0=q0:
                                         e.tensor_tensor(
                                    apply_dst_fn(cc)[:, h0 + q0:h0 + q0 + 512], sB,
                                    rh[:, q0:q0 + 512], ALU.add),
                                    deps=[t1, o7])
                                apply_war[slot] = [t2]
                                apply_ops[cc].append(t2)
                                apply_tt[(cc, half * 2 + q)] = t2
                gate["ps"] = list(stats_war)
            return (apply_ops if apply_dst_fn is not None else None), apply_tt, bc_done

        def proj_fm(wdram, col0, rhs_fn, rhs_deps_fn, evac_fn, ps_proj, war_fn=None,
                    second=None, nreg=2, pre_marks=None, pre_dg2=0):
            out_ops = {}
            cache_users = g_cache
            region_users = {}
            piece = 0
            stg_cnt = 0
            nW = 2 if second is not None else 1
            for dg2 in range(4):
                use_mark = pre_marks is not None and dg2 < pre_dg2
                for wi in range(nW):
                    wd, c0 = (wdram, col0) if wi == 0 else (second[0], second[1])
                    key = (dg2 % 2) if nW == 1 else wi
                    consumed = cache_users[key]
                    cache_users[key] = []
                    for c in range(CH):
                        slot = stg_cnt % 4
                        stg_cnt += 1
                        stv = w_stg[:, slot * 256:slot * 256 + 256]
                        dma = s.dma_sq(lambda e, c=c, dg2=dg2, stv=stv, wd=wd, c0=c0: e.dma_start(
                            out=stv, in_=wd[c * 128:(c + 1) * 128,
                                            c0 + dg2 * 256:c0 + dg2 * 256 + 256]),
                            f"wstg{slot}", deps=g_stg[slot])
                        if nW == 1:
                            dstv = w_cache[:, (dg2 % 2) * 2048 + c * 256:(dg2 % 2) * 2048 + c * 256 + 256]
                        else:
                            dstv = w_cache[:, wi * 2048 + c * 256:wi * 2048 + c * 256 + 256]
                        ctx = s.at_marks(pre_marks) if use_mark else contextlib.nullcontext()
                        with ctx:
                            if (c + dg2) % 2 == 0:
                                cst = s.dv(lambda e, stv=stv, dstv=dstv: e.tensor_copy(dstv, stv),
                                           deps=[dma] + consumed)
                            else:
                                cst = s.ac(lambda e, stv=stv, dstv=dstv: e.activation(
                                    dstv, stv, AF.Identity), deps=[dma] + consumed)
                        g_stg[slot] = [cst]
                        out_ops.setdefault(("cast", (dg2, wi)), {})[c] = cst
                for tt in range(4):
                    par = piece % nreg
                    for wi in range(nW):
                        rfn = rhs_fn if wi == 0 else second[2]
                        rdeps = rhs_deps_fn if wi == 0 else second[3]
                        efn = evac_fn if wi == 0 else second[4]
                        psp = ps_proj if wi == 0 else second[5]
                        mms_dd = {0: [], 1: []}
                        for ci in range(CH):
                            for dd in range(2):
                                pp = psp[:, par * 1024 + dd * 512: par * 1024 + dd * 512 + 512]
                                deps = [out_ops[("cast", (dg2, wi))][ci]] + rdeps(ci, tt)
                                if ci == 0:
                                    deps = deps + region_users.get((wi, par), gate["ps"])
                                    if war_fn is not None and wi == 0:
                                        deps = deps + war_fn(dg2 * 2 + dd, tt)
                                if nW == 1:
                                    wv = w_cache[:, (dg2 % 2) * 2048 + ci * 256 + dd * 128:
                                                 (dg2 % 2) * 2048 + ci * 256 + dd * 128 + 128]
                                else:
                                    wv = w_cache[:, wi * 2048 + ci * 256 + dd * 128:
                                                 wi * 2048 + ci * 256 + dd * 128 + 128]
                                mm = s.pe(lambda e, wv=wv, pp=pp, ci=ci, tt=tt, rfn=rfn: e.matmul(
                                    pp, wv, rfn(ci)[:, tt * 512:(tt + 1) * 512],
                                    start=(ci == 0), stop=(ci == CH - 1)), deps=deps)
                                mms_dd[dd].append(mm)
                                cache_users[(dg2 % 2) if nW == 1 else wi].append(mm)
                        evs_all = []
                        for dd in range(2):
                            oc = dg2 * 2 + dd
                            pp = psp[:, par * 1024 + dd * 512: par * 1024 + dd * 512 + 512]
                            evs = efn(oc, tt, pp, mms_dd[dd][-1], par)
                            out_ops[(wi, oc, tt)] = evs
                            evs_all += evs
                        region_users[(wi, par)] = evs_all
                    piece += 1
            gate["ps"] = [op for v in region_users.values() for op in v]
            return out_ops

        def proj_seq(specs, ps, nreg=4, post_tile=None):
            out_ops = {}
            region_users = {}
            piece = 0
            stg_cnt = 0
            tile_idx = 0
            for dg2 in range(4):
                for si, (wd, c0, rhs_fn, rdeps_fn, evac_fn) in enumerate(specs):
                    key = tile_idx % 2
                    consumed = g_cache[key]
                    g_cache[key] = []
                    casts = {}
                    for c in range(CH):
                        slot = stg_cnt % 4
                        stg_cnt += 1
                        stv = w_stg[:, slot * 256:slot * 256 + 256]
                        dma = s.dma_sq(lambda e, c=c, dg2=dg2, stv=stv, wd=wd, c0=c0:
                                       e.dma_start(
                            out=stv, in_=wd[c * 128:(c + 1) * 128,
                                            c0 + dg2 * 256:c0 + dg2 * 256 + 256]),
                            f"wstg{slot}", deps=g_stg[slot])
                        dstv = w_cache[:, key * 2048 + c * 256:key * 2048 + c * 256 + 256]
                        if (c + tile_idx) % 2 == 0:
                            cst = s.gp(lambda e, stv=stv, dstv=dstv: e.tensor_copy(
                                dstv, stv), deps=[dma] + consumed)
                        else:
                            cst = s.ac(lambda e, stv=stv, dstv=dstv: e.activation(
                                dstv, stv, AF.Identity), deps=[dma] + consumed)
                        g_stg[slot] = [cst]
                        casts[c] = cst
                    mm_tile = []
                    for tt in range(4):
                        par = piece % nreg
                        mms_dd = {0: [], 1: []}
                        for ci in range(CH):
                            for dd in range(2):
                                pp = ps[:, par * 1024 + dd * 512:par * 1024 + dd * 512 + 512]
                                deps = [casts[ci]] + rdeps_fn(ci, tt)
                                if ci == 0:
                                    deps = deps + region_users.get(par, gate["ps"])
                                wv = w_cache[:, key * 2048 + ci * 256 + dd * 128:
                                             key * 2048 + ci * 256 + dd * 128 + 128]
                                mm = s.pe(lambda e, wv=wv, pp=pp, ci=ci, tt=tt,
                                          rhs_fn=rhs_fn: e.matmul(
                                    pp, wv, rhs_fn(ci)[:, tt * 512:(tt + 1) * 512],
                                    start=(ci == 0), stop=(ci == CH - 1)), deps=deps)
                                mms_dd[dd].append(mm)
                                mm_tile.append(mm)
                                g_cache[key].append(mm)
                        evs_all = []
                        for dd in range(2):
                            oc = dg2 * 2 + dd
                            pp = ps[:, par * 1024 + dd * 512:par * 1024 + dd * 512 + 512]
                            evs = evac_fn(oc, tt, pp, mms_dd[dd][-1], par)
                            out_ops[(si, oc, tt)] = evs
                            evs_all += evs
                        region_users[par] = evs_all
                        piece += 1
                    out_ops[("mm", si, dg2)] = mm_tile
                    tile_idx += 1
                    if post_tile is not None:
                        post_tile(si, dg2)
            gate["ps"] = [op for v in region_users.values() for op in v]
            return out_ops

        a_es = contextlib.ExitStack()
        bcs_a = a_es.enter_context(nc.sbuf_tensor(_nm("bcs_a"), [128, T], F32))
        bch_a = a_es.enter_context(nc.sbuf_tensor(_nm("bch_a"), [128, T], F32))
        xsc_a = a_es.enter_context(nc.sbuf_tensor(_nm("xsc_a"), [128, 4 * HF], F32))
        src_a = mk_dram_src(d_x, xsc_a)
        applyA, applyA_tt, _ = emit_ln(src_a, bcs_a, bch_a, cast_dst_fn=xnv,
                                       apply_dst_fn=xnv, apply_from_cast=True,
                                       apply_gp_cut=5)
        a_es.close()

        st_es = contextlib.ExitStack()
        states = st_es.enter_context(nc.sbuf_tensor(_nm("states"), [128, 12288], F32))
        gate_bf = states.ap().bitcast(BF16)

        def stv(cc):
            return states[:, 8192 + (cc % 2) * T:8192 + (cc % 2) * T + T]

        def ygv(cc):
            return gate_bf[:, cc * T:(cc + 1) * T]

        v_ops = {}
        y_ops = {}
        sil_ops = {}
        scan_ops = {}
        with contextlib.ExitStack() as les:
            ps_a = les.enter_context(nc.psum_tensor(_nm("ps_vqg"), [128, 4096], F32))

            def ev_v(oc, tt, pp, mml, par):
                ev = s.ac(lambda e, oc=oc, tt=tt, pp=pp: e.activation(
                    arv(oc)[:, tt * 512:(tt + 1) * 512], pp, AF.Identity), deps=[mml])
                v_ops[(oc, tt)] = ev
                return [ev]

            def ev_q(oc, tt, pp, mml, par):
                ev = s.dv(lambda e, oc=oc, tt=tt, pp=pp: e.tensor_tensor(
                    arv(oc)[:, tt * 512:(tt + 1) * 512], pp,
                    stv(oc)[:, tt * 512:(tt + 1) * 512], ALU.mult),
                    deps=[mml, scan_ops[oc]])
                y_ops[(oc, tt)] = ev
                return [ev]

            def ev_g(oc, tt, pp, mml, par):
                ev = s.ac(lambda e, oc=oc, tt=tt, pp=pp: e.activation(
                    ygv(oc)[:, tt * 512:(tt + 1) * 512], pp, AF.Silu),
                    deps=[mml] + [y_ops[(oc // 2, t4)] for t4 in range(4)])
                sil_ops[(oc, tt)] = ev
                return [ev]

            def post_tile(si, dg2):
                if si == 0:
                    for dd in range(2):
                        oc = dg2 * 2 + dd
                        war = ([y_ops[(oc - 2, tt)] for tt in range(4)]
                               if oc >= 2 else [])
                        scan_ops[oc] = s.dv(lambda e, oc=oc: e.tensor_tensor_scan(
                            stv(oc), lam_c[:, oc:oc + 1].broadcast_to([128, T]),
                            arv(oc), 0.0, ALU.mult, ALU.add),
                            deps=[v_ops[(oc, tt)] for tt in range(4)] + [lam_op] + war)

            seq_ops = proj_seq(
                [(dw["wvT"], 0, xnv, lambda ci, tt: [applyA_tt[(ci, tt)]], ev_v),
                 (dw["wqT"], 0, xnv, lambda ci, tt: [applyA_tt[(ci, tt)]], ev_q),
                 (dw["wgT"], 0, xnv, lambda ci, tt: [applyA_tt[(ci, tt)]], ev_g)],
                ps_a, nreg=4, post_tile=post_tile)

        marks_wo = s.mark()

        yg_es = contextlib.ExitStack()
        bcs_y = yg_es.enter_context(nc.sbuf_tensor(_nm("bcs_y"), [128, T], F32))
        bch_y = yg_es.enter_context(nc.sbuf_tensor(_nm("bch_y"), [128, T], F32))
        src_y = mk_sbuf_src(arv, lambda cc: [y_ops[(cc, tt)] for tt in range(4)])
        all_yops = [y_ops[(c, t)] for c in range(CH) for t in range(4)]
        _, _, bc_y = emit_ln(src_y, bcs_y, bch_y, cast_dst_fn=xnv,
                             extra_src_deps_fn=lambda half: all_yops
                             + seq_ops[("mm", 2, 3)])

        yg_ops = {}
        y_free = {}
        gate_all = []
        for tt in range(4):
            for cc in range(CH):
                sl = slice(tt * 512, (tt + 1) * 512)
                pslot = (tt * CH + cc) % 4
                sB = p_scr[:, (pslot * 2) * 512:(pslot * 2) * 512 + 512]
                sC = p_scr[:, (pslot * 2 + 1) * 512:(pslot * 2 + 1) * 512 + 512]
                t1 = s.dv(lambda e, cc=cc, sl=sl, sB=sB: e.tensor_tensor(
                    sB, arv(cc)[:, sl], bcs_y[:, sl], ALU.mult),
                    deps=[y_ops[(cc, tt)]] + bc_y + y_free.get((pslot, 0), []))
                t2 = s.dv(lambda e, sl=sl, sB=sB, sC=sC: e.tensor_tensor(
                    sC, sB, bch_y[:, sl], ALU.add), deps=[t1] + y_free.get((pslot, 1), []))
                yn = s.ac(lambda e, cc=cc, sB=sB, sC=sC: e.activation(
                    sB, sC, AF.Identity, bias=betc[:, cc:cc + 1], scale=gamc[:, cc:cc + 1]),
                    deps=[t2, ld["gamc"], ld["betc"]])
                yg = s.gp(lambda e, cc=cc, sl=sl, sB=sB: e.tensor_tensor(
                    ygv(cc)[:, sl], sB, ygv(cc)[:, sl], ALU.mult),
                    deps=[yn, sil_ops[(cc, tt)]])
                yg_ops[(cc, tt)] = yg
                y_free[(pslot, 0)] = [yg]
                y_free[(pslot, 1)] = [yn]
                gate_all += [t1, t2]
        yg_es.close()

        res1_ops = {}
        with contextlib.ExitStack() as les:
            ps_proj = les.enter_context(nc.psum_tensor(_nm("ps_wa"), [128, 4096], F32))
            xsc2 = les.enter_context(nc.sbuf_tensor(_nm("xsc2_a"), [128, 2 * T], F32))
            xdma = {}

            def ev_wo(oc, tt, pp, mml, par):
                slot = oc % 2
                chan = f"str{slot}"
                if oc not in xdma:
                    xdma[oc] = s.dma_gp(lambda e, oc=oc, slot=slot: e.dma_start(
                        out=xsc2[:, slot * T:slot * T + T],
                        in_=d_x[oc * 128:(oc + 1) * 128, :]), chan,
                        deps=g_chan.get(chan, []) + gate_all)
                    g_chan[chan] = []
                ev = s.dv(lambda e, oc=oc, tt=tt, pp=pp, slot=slot: e.tensor_tensor(
                    arv(oc)[:, tt * 512:(tt + 1) * 512],
                    xsc2[:, slot * T + tt * 512: slot * T + (tt + 1) * 512], pp, ALU.add),
                    deps=[mml, xdma[oc], yg_ops[(oc, tt)]])
                res1_ops[(oc, tt)] = ev
                g_chan[chan].append(ev)
                return [ev]

            proj_fm(dw["woT"], 0, ygv, lambda ci, tt: [yg_ops[(ci, tt)]],
                    ev_wo, ps_proj, nreg=4, pre_marks=marks_wo, pre_dg2=2)
        st_es.close()

        def x1_ready(cc):
            return [res1_ops[(cc, tt)] for tt in range(4)]

        def ffn_layer(resid_in_ops):
            marks_f = s.mark()
            f_es = contextlib.ExitStack()
            with f_es:
                bcs = f_es.enter_context(nc.sbuf_tensor(_nm("bcs_f"), [128, T], F32))
                bch = f_es.enter_context(nc.sbuf_tensor(_nm("bch_f"), [128, T], F32))
                src = mk_sbuf_src(arv, lambda cc: resid_in_ops(cc))
                applyF, applyF_tt, _ = emit_ln(
                    src, bcs, bch, cast_dst_fn=xnv, apply_dst_fn=xnv,
                    extra_src_deps_fn=lambda half: [
                        op for c in range(CH) for op in resid_in_ops(c)])
            m_es = contextlib.ExitStack()
            xr = m_es.enter_context(nc.sbuf_tensor(_nm("xr_f"), [128, CH * T], BF16))
            xk = m_es.enter_context(nc.sbuf_tensor(_nm("xk_f"), [128, CH * T], BF16))
            d_s = m_es.enter_context(nc.sbuf_tensor(_nm("d_f"), [128, 3072], BF16))

            def xrv(cc):
                return xr[:, cc * T:(cc + 1) * T]

            def xkv(cc):
                return xk[:, cc * T:(cc + 1) * T]

            mix_r = {}
            mix_k = {}
            mix_all = {}
            d_war = {i: [] for i in range(4)}
            for g in (1, 2):
                for i in range(4):
                    d_war[(g, i)] = []
            for cc in range(CH):
                for tt in range(4):
                    t0 = tt * 512
                    hi = t0 + 512
                    adep = [applyF_tt[(cc, tt)]]
                    if tt > 0:
                        adep.append(applyF_tt[(cc, tt - 1)])
                    sl = (cc + tt) % 2
                    dsv = d_s[:, sl * 512:sl * 512 + 512]
                    dops = []
                    if tt == 0:
                        dops.append(s.gp(lambda e, cc=cc, dsv=dsv: e.tensor_copy(
                            dsv[:, 0:1], xnv(cc)[:, 0:1]), deps=adep + d_war[sl]))
                        dops.append(s.gp(lambda e, cc=cc, dsv=dsv: e.tensor_tensor(
                            dsv[:, 1:512], xnv(cc)[:, 1:512], xnv(cc)[:, 0:511],
                            ALU.subtract), deps=adep + d_war[sl]))
                    else:
                        dops.append(s.gp(lambda e, cc=cc, dsv=dsv, t0=t0, hi=hi:
                                         e.tensor_tensor(
                            dsv[:], xnv(cc)[:, t0:hi], xnv(cc)[:, t0 - 1:hi - 1],
                            ALU.subtract), deps=adep + d_war[sl]))
                    mrv = d_s[:, 1024 + sl * 512:1024 + sl * 512 + 512]
                    mkv = d_s[:, 2048 + sl * 512:2048 + sl * 512 + 512]
                    mr = s.dv(lambda e, dsv=dsv, mrv=mrv, cc=cc: e.tensor_scalar_mul(
                        mrv, dsv[:], murc[:, cc:cc + 1]),
                        deps=dops + [ld["murc"]] + d_war[(1, sl)])
                    mk = s.dv(lambda e, dsv=dsv, mkv=mkv, cc=cc: e.tensor_scalar_mul(
                        mkv, dsv[:], mukc[:, cc:cc + 1]),
                        deps=dops + [ld["mukc"]] + d_war[(2, sl)])
                    r1 = s.dv(lambda e, cc=cc, mrv=mrv, t0=t0, hi=hi: e.tensor_tensor(
                        xrv(cc)[:, t0:hi], mrv, xnv(cc)[:, t0:hi], ALU.add),
                        deps=[mr])
                    k1 = s.dv(lambda e, cc=cc, mkv=mkv, t0=t0, hi=hi: e.tensor_tensor(
                        xkv(cc)[:, t0:hi], mkv, xnv(cc)[:, t0:hi], ALU.add),
                        deps=[mk])
                    mix_r[(cc, tt)] = [r1]
                    mix_k[(cc, tt)] = [k1]
                    mix_all[(cc, tt)] = dops + [r1, k1]
                    d_war[sl] = [mr, mk]
                    d_war[(1, sl)] = [r1]
                    d_war[(2, sl)] = [k1]

            kact_ops = {}
            k_scratch = []
            with contextlib.ExitStack() as les:
                ps_proj = les.enter_context(nc.psum_tensor(_nm("ps_k"), [128, 4096], F32))

                def ev_k(oc, tt, pp, mml, par):
                    sB = p_scr[:, par * 1024:par * 1024 + 512]
                    t1 = s.dv(lambda e, pp=pp, sB=sB: e.tensor_scalar_max(sB, pp, 0.0),
                              deps=[mml] + kact_ops.get(("free", par), []))
                    t2 = s.ac(lambda e, oc=oc, tt=tt, sB=sB: e.activation(
                        xnv(oc)[:, tt * 512:(tt + 1) * 512], sB, AF.Square),
                        deps=[t1] + mix_all[(oc, tt)]
                        + (mix_all[(oc, tt + 1)] if tt < 3 else []))
                    kact_ops[(oc, tt)] = t2
                    kact_ops[("free", par)] = [t2]
                    k_scratch.append(t2)
                    return [t1, t2]

                proj_fm(dw["wkT"], 0, xkv, lambda ci, tt: mix_k[(ci, tt)],
                        ev_k, ps_proj, nreg=4, pre_marks=marks_f, pre_dg2=2)

            res_ops = {}
            with contextlib.ExitStack() as les:
                ps_r = les.enter_context(nc.psum_tensor(_nm("ps_r"), [128, 2048], F32))
                ps_v = les.enter_context(nc.psum_tensor(_nm("ps_v2"), [128, 2048], F32))
                t3_scr = p_scr[:, 3072:4096]
                sig_ops = {}

                def ev_r(oc, tt, pp, mml, par):
                    dd = oc % 2
                    sB = p_scr[:, (par * 2 + dd) * 512:(par * 2 + dd) * 512 + 512]
                    sg = s.ac(lambda e, pp=pp, sB=sB: e.activation(sB, pp, AF.Sigmoid),
                              deps=[mml] + sig_ops.get(("free", par, dd), k_scratch))
                    sig_ops[(oc, tt)] = (sg, sB)
                    return [sg]

                def ev_v2(oc, tt, pp, mml, par):
                    dd = oc % 2
                    sg, sB = sig_ops[(oc, tt)]
                    t3 = t3_scr[:, dd * 512:(dd + 1) * 512]
                    m1 = s.dv(lambda e, pp=pp, sB=sB, t3=t3: e.tensor_tensor(
                        t3, sB, pp, ALU.mult), deps=[mml, sg] + res_ops.get(("free", dd), []))
                    sig_ops[("free", par, dd)] = [m1]
                    rr = s.gp(lambda e, oc=oc, tt=tt, t3=t3: e.tensor_tensor(
                        arv(oc)[:, tt * 512:(tt + 1) * 512],
                        arv(oc)[:, tt * 512:(tt + 1) * 512], t3, ALU.add),
                        deps=[m1] + resid_in_ops(oc))
                    res_ops[(oc, tt)] = rr
                    res_ops[("free", dd)] = [rr]
                    return [m1]

                proj_fm(dw["wrT"], 0, xrv, lambda ci, tt: mix_r[(ci, tt)],
                        ev_r, ps_r,
                        second=(dw["wfvT"], 0, xnv,
                                lambda ci, tt: [kact_ops[(ci, tt)]],
                                ev_v2, ps_v))
            m_es.close()
            return lambda cc: [res_ops[(cc, tt)] for tt in range(4)]

        x2_ready = ffn_layer(x1_ready) if stop_after >= 2 else x1_ready

        if stop_after >= 3:
            b_es = contextlib.ExitStack()
            bcs_b = b_es.enter_context(nc.sbuf_tensor(_nm("bcs_b"), [128, T], F32))
            bch_b = b_es.enter_context(nc.sbuf_tensor(_nm("bch_b"), [128, T], F32))
            marks_b = s.mark()
            src_b = mk_sbuf_src(arv, lambda cc: x2_ready(cc))
            all_res_b = [op for c in range(CH) for op in x2_ready(c)]
            applyB, applyB_tt, _ = emit_ln(src_b, bcs_b, bch_b, cast_dst_fn=xnv,
                                           apply_dst_fn=xnv,
                                           extra_src_deps_fn=lambda half: all_res_b,
                                           apply_gp_cut=4)
            b_es.close()
            spill = {}
            for cc in range(CH):
                spill[cc] = s.dma_gp(lambda e, cc=cc: e.dma_start(
                    out=d_xs1[cc * 128:(cc + 1) * 128, :], in_=arv(cc)),
                    f"spill{cc}", deps=x2_ready(cc) + applyB[cc])

            def qv(cc):
                return arena_bf[:, cc * T:(cc + 1) * T]

            def kv(cc):
                return arena_bf[:, (CH + cc) * T:(CH + cc + 1) * T]

            q_ops = {}
            k_ops = {}
            qk_es = contextlib.ExitStack()
            v_tok = qk_es.enter_context(nc.sbuf_tensor(_nm("v_tok"), [128, 16 * 1024], BF16))
            vs_es = contextlib.ExitStack()
            v_scr = vs_es.enter_context(nc.sbuf_tensor(_nm("v_scr"), [128, 2 * T], BF16))
            vt_r = v_tok.ap().rearrange("p (j c) -> p j c", c=1024)
            vtok_ready = {}
            with contextlib.ExitStack() as les:
                ps_proj = les.enter_context(nc.psum_tensor(_nm("ps_q"), [128, 2048], F32))

                def ev_qb(oc, tt, pp, mml, par):
                    ev = s.ac(lambda e, oc=oc, tt=tt, pp=pp: e.activation(
                        qv(oc)[:, tt * 512:(tt + 1) * 512], pp, AF.Identity, scale=QSCALE),
                        deps=[mml, spill[oc // 2]])
                    q_ops[(oc, tt)] = ev
                    return [ev]

                proj_fm(d_wqkv, 0, xnv, lambda ci, tt: [applyB_tt[(ci, tt)]], ev_qb, ps_proj,
                        pre_marks=marks_b, pre_dg2=2)

                def ev_kb(oc, tt, pp, mml, par):
                    ev = s.ac(lambda e, oc=oc, tt=tt, pp=pp: e.activation(
                        kv(oc)[:, tt * 512:(tt + 1) * 512], pp, AF.Identity),
                        deps=[mml, spill[4 + oc // 2]])
                    k_ops[(oc, tt)] = ev
                    return [ev]

                proj_fm(d_wqkv, D, xnv, lambda ci, tt: [applyB_tt[(ci, tt)]], ev_kb, ps_proj)

                ps_tr = les.enter_context(nc.psum_tensor(_nm("ps_tr"), [128, 2048], BF16))
                vscr_ops = {}
                vscr_war = {0: [], 1: []}
                tr_war = {i: list(gate["ps"]) for i in range(8)}

                def ev_vb(oc, tt, pp, mml, par):
                    slot = oc % 2
                    ev = s.ac(lambda e, oc=oc, tt=tt, pp=pp, slot=slot: e.activation(
                        v_scr[:, slot * T + tt * 512: slot * T + (tt + 1) * 512], pp,
                        AF.Identity), deps=[mml] + vscr_war[slot])
                    vscr_ops.setdefault(oc, []).append(ev)
                    ret = [ev]
                    if tt == 3:
                        war_new = []
                        for g in range(4):
                            trs = []
                            for j in range(4):
                                j0 = g * 4 + j
                                pslot = (j0 % 8) * 256
                                tr = s.pe(lambda e, oc=oc, slot=slot, j0=j0, pslot=pslot: e.transpose(
                                    ps_tr[:, pslot:pslot + 128],
                                    v_scr[:, slot * T + j0 * 128: slot * T + (j0 + 1) * 128],
                                    idn[:]),
                                    deps=vscr_ops[oc] + [ld["idn"]] + tr_war.get(j0 % 8, []))
                                trs.append(tr)
                            base = (g * 4) % 8
                            cp = s.dv(lambda e, oc=oc, g=g, base=base: e.tensor_copy(
                                vt_r[:, g * 4:(g + 1) * 4, oc * 128:(oc + 1) * 128],
                                ps_tr[:, base * 256: base * 256 + 1024]
                                .rearrange("p (j c) -> p j c", c=256)[:, :, 0:128]), deps=trs)
                            for j in range(4):
                                tr_war[(g * 4 + j) % 8] = [cp]
                            vtok_ready[(oc, g)] = cp
                            war_new.append(cp)
                        vscr_war[slot] = war_new
                    return ret

                proj_fm(d_wqkv, 2 * D, xnv, lambda ci, tt: [applyB_tt[(ci, tt)]], ev_vb, ps_proj)

            vs_es.close()

            n_es = contextlib.ExitStack()
            negM = n_es.enter_context(nc.sbuf_tensor(_nm("negM"), [128, 16], F32))
            mrow = n_es.enter_context(nc.sbuf_tensor(_nm("mrow"), [16, 4], F32))
            mrow_bf = n_es.enter_context(nc.sbuf_tensor(_nm("mrow_bf"), [16, 1], BF16))
            nrow_bf = n_es.enter_context(nc.sbuf_tensor(_nm("nrow_bf"), [1, 16], BF16))
            with contextlib.ExitStack() as les:
                ps_m = les.enter_context(nc.psum_tensor(_nm("ps_m"), [16, 2048], F32))
                ps_n1 = les.enter_context(nc.psum_tensor(_nm("ps_n1"), [1, 1024], BF16))
                ps_n2 = les.enter_context(nc.psum_tensor(_nm("ps_n2"), [128, 512], F32))
                red_ops = []
                for qk in range(2):
                    view = qv if qk == 0 else kv
                    vops = q_ops if qk == 0 else k_ops
                    mm_all = []
                    sq_prev = []
                    for cc in range(CH):
                        sqo = s.ac(lambda e, cc=cc, view=view: e.activation(
                            sq_scr[:, 0:T], view(cc), AF.Square),
                            deps=[vops[(cc, tt)] for tt in range(4)] + sq_prev)
                        mms = []
                        for ip in range(4):
                            mms.append(s.pe(lambda e, cc=cc, ip=ip: e.matmul(
                                ps_m[:, ip * 512:(ip + 1) * 512],
                                Gind[:, 14 - 2 * cc:30 - 2 * cc],
                                sq_scr[:, ip * 512:(ip + 1) * 512],
                                start=(cc == 0), stop=(cc == CH - 1)),
                                deps=[sqo, ld["Gind"]] + red_ops
                                + (gate["ps"] if cc == 0 else [])))
                        sq_prev = mms
                        mm_all = mms
                    red = s.dv(lambda e, qk=qk: e.reduce_max(
                        mrow[:, qk:qk + 1], ps_m[:, 0:2048], axis=AX.X), deps=mm_all)
                    sq = s.ac(lambda e, qk=qk: e.activation(
                        mrow[:, 2 + qk:3 + qk], mrow[:, qk:qk + 1], AF.Sqrt), deps=[red])
                    red_ops = [red, sq]
                nm = s.dv(lambda e: e.tensor_scalar(
                    mrow[:, 0:1], mrow[:, 2:3], mrow[:, 3:4], -1.0, ALU.mult, ALU.mult),
                    deps=red_ops)
                nmb = s.dv(lambda e: e.tensor_copy(mrow_bf[:], mrow[:, 0:1]), deps=[nm])
                trn = s.pe(lambda e: e.transpose(ps_n1[0:1, 0:16], mrow_bf[:],
                                                 idn[0:16, 0:16]),
                           deps=[nmb, ld["idn"]] + list(vtok_ready.values()))
                nrw = s.dv(lambda e: e.tensor_copy(nrow_bf[:], ps_n1[0:1, 0:16]), deps=[trn])
                bco = []
                for h in range(16):
                    bm = s.pe(lambda e, h=h: e.matmul(
                        ps_n2[:, h:h + 1], ones_rb[:], nrow_bf[0:1, h:h + 1],
                        start=True, stop=True), deps=[nrw, c_rb])
                    bco.append(bm)
                negM_op = s.dv(lambda e: e.tensor_copy(negM[:], ps_n2[:, 0:16]), deps=bco)
                gate["ps"] = [negM_op, nrw] + red_ops

            at_es = contextlib.ExitStack()
            Dbuf = at_es.enter_context(nc.sbuf_tensor(_nm("Dbuf"), [128, 4 * T], BF16))
            Ebuf = at_es.enter_context(nc.sbuf_tensor(_nm("Ebuf"), [128, 4096], BF16))
            Ebuf2 = at_es.enter_context(nc.sbuf_tensor(_nm("Ebuf2"), [128, 4096], BF16))
            vhat = at_es.enter_context(nc.sbuf_tensor(_nm("vhat"), [128, 32 * 66], BF16))
            vhat_i32 = vhat.ap().bitcast(mybir.dt.int32).rearrange(
                "p (j c) -> p j c", c=33)
            vt_i32 = v_tok.ap().bitcast(mybir.dt.int32).rearrange(
                "p (j c) -> p j c", c=512)
            rsum = [[ln_scr[64 * ln:64 * ln + 1, r * 1024:(r + 1) * 1024]
                     for r in range(2)] for ln in range(2)]
            rlbf = [[sq_scr[0:1, ln * 1024:(ln + 1) * 1024]
                     for r in range(2)] for ln in range(2)]
            rlbc = [p_scr[ln * 64:(ln + 1) * 64, 2048:3072] for ln in range(2)]
            vtr = list(vtok_ready.values())
            ld_D = {}
            d_war = {0: [], 1: []}

            def dview(cc, lane):
                return Dbuf[:, ((cc % 2) * 2 + lane) * T:((cc % 2) * 2 + lane) * T + T]
            vh_ms = s.dv(lambda e: e.memset(vhat[:], 1.0))
            out_ready = {}
            marks_w = s.mark()
            with contextlib.ExitStack() as les:
                psS = [les.enter_context(nc.psum_tensor(_nm("ps_Sa"), [128, 1024], F32)),
                       les.enter_context(nc.psum_tensor(_nm("ps_Sb"), [128, 1024], F32))]
                psU = [les.enter_context(nc.psum_tensor(_nm("ps_Ua"), [65, 1024], F32)),
                       les.enter_context(nc.psum_tensor(_nm("ps_Ub"), [65, 1024], F32))]
                S_war = {ln: list(gate["ps"]) for ln in range(2)}
                E_war = {(ln, sl): [] for ln in range(2) for sl in range(2)}
                E2_war = {(ln, sl): [] for ln in range(2) for sl in range(2)}
                stt_war = {(ln, sl): [] for ln in range(2) for sl in range(2)}
                U_war = {0: list(gate["ps"]), 1: list(gate["ps"])}
                vh_war = {0: [vh_ms], 1: [vh_ms]}
                vcp = {}
                rsum_war = {(ln, r): [] for ln in range(2) for r in range(2)}
                rlbf_war = {ln: [] for ln in range(2)}
                rlbc_war = {0: [], 1: []}
                defer = {0: None, 1: None}
                it_idx = [0]

                def emit_deferred(lane):
                    if defer[lane] is not None:
                        defer[lane]()
                        defer[lane] = None

                def mk_defer(cc, lane, ihalf, smc, ev0, ring):
                    def emit():
                        rs_v = rsum[lane][ring]
                        rb_v = rlbf[lane][ring]
                        ln0 = s.ac(lambda e, rs_v=rs_v: e.activation(
                            rs_v, rs_v, AF.Ln), deps=[smc])
                        rex = s.ac(lambda e, rs_v=rs_v: e.activation(
                            rs_v, rs_v, AF.Exp, scale=-1.0), deps=[ln0])
                        rlb = s.dv(lambda e, rs_v=rs_v, rb_v=rb_v: e.tensor_copy(
                            rb_v, rs_v), deps=[rex] + rlbf_war[lane])
                        rsum_war[(lane, ring)] = [rlb]
                        bms = []
                        for ip in range(2):
                            bm = s.pe(lambda e, lane=lane, ip=ip, rb_v=rb_v: e.matmul(
                                psU[lane][0:64, ip * 512:(ip + 1) * 512],
                                ones_rb[0:1, 0:64], rb_v[0:1, ip * 512:(ip + 1) * 512],
                                start=True, stop=True),
                                deps=[rlb, c_rb, ev0, smc] + rlbc_war[lane])
                            bms.append(bm)
                        rlbf_war[lane] = bms
                        rbe = s.dv(lambda e, lane=lane: e.tensor_copy(
                            rlbc[lane][:], psU[lane][0:64, 0:1024]), deps=bms)
                        U_war[lane] = U_war[lane] + [rbe]
                        i0 = ihalf * 1024
                        mul = s.dv(lambda e, cc=cc, lane=lane, i0=i0: e.tensor_tensor(
                            qv(cc)[lane * 64:(lane + 1) * 64, i0:i0 + 1024],
                            qv(cc)[lane * 64:(lane + 1) * 64, i0:i0 + 1024],
                            rlbc[lane][:], ALU.mult), deps=[rbe, ev0])
                        rlbc_war[lane] = [mul]
                        out_ready[(cc, lane, ihalf)] = mul
                    return emit

                for cc in range(CH):
                    if cc not in ld_D:
                        slot = cc % 2
                        dl_ = []
                        for lane in range(2):
                            h = 2 * cc + lane
                            dl_.append(s.dma_gp(lambda e, h=h, lane=lane, slot=slot: e.dma_start(
                                out=Dbuf[:, (slot * 2 + lane) * T:(slot * 2 + lane) * T + T],
                                in_=d_Dm[h * 128:(h + 1) * 128, :]),
                                f"dm{slot}{lane}", deps=d_war[slot] + (vtr if cc < 2 else [])))
                        ld_D[cc] = dl_
                    for lane in range(2):
                        h = 2 * cc + lane
                        vcp[(cc, lane)] = s.dv(lambda e, h=h, lane=lane: e.tensor_copy(
                            vhat_i32[:, lane * 16:(lane + 1) * 16, 0:32],
                            vt_i32[:, :, h * 32:h * 32 + 32]),
                            deps=[vtok_ready[(cc, g)] for g in range(4)] + vh_war[lane])
                        vh_war[lane] = []
                    for ihalf in range(2):
                        i0 = ihalf * 1024
                        nj = 8 if ihalf == 0 else 16
                        pend = {0: None, 1: None}
                        pv_ops = {0: [], 1: []}
                        for j0 in range(nj):
                            jb = 128 * j0
                            lo0 = max(jb, i0)
                            wtot = i0 + 1024 - lo0
                            sl = j0 % 2
                            smm = {}
                            for lane in range(2):
                                mlist = []
                                for kk in range(2):
                                    blo = max(lo0, i0 + kk * 512)
                                    bhi = i0 + (kk + 1) * 512
                                    if blo >= bhi:
                                        continue
                                    mlist.append(s.pe(lambda e, cc=cc, lane=lane, jb=jb,
                                                      blo=blo, bhi=bhi, i0=i0: e.matmul(
                                        psS[lane][:, blo - i0:bhi - i0],
                                        kv(cc)[lane * 64:(lane + 1) * 64, jb:jb + 128],
                                        qv(cc)[lane * 64:(lane + 1) * 64, blo:bhi],
                                        start=True, stop=True, tile_position=(lane * 64, 0)),
                                        deps=[k_ops[(cc, jb // 512)], q_ops[(cc, blo // 512)],
                                              q_ops[(cc, (bhi - 1) // 512)]] + S_war[lane]))
                                smm[lane] = mlist
                            if j0 == 1:
                                emit_deferred(0)
                                emit_deferred(1)
                            for lane in range(2):
                                if pend[lane] is not None:
                                    pend[lane]()
                                    pend[lane] = None
                            for lane in range(2):
                                h = 2 * cc + lane
                                ebase = (lane * 2 + sl) * 1024
                                ex = s.ac(lambda e, ebase=ebase, wtot=wtot, h=h, lane=lane,
                                          lo0=lo0, i0=i0: e.activation(
                                    Ebuf[:, ebase:ebase + wtot],
                                    psS[lane][:, lo0 - i0:1024],
                                    AF.Exp, bias=negM[:, h:h + 1]),
                                    deps=smm[lane] + [negM_op] + E_war[(lane, sl)])
                                S_war[lane] = [ex]
                                dm = s.dv(lambda e, ebase=ebase, wtot=wtot, cc=cc, lane=lane,
                                          lo0=lo0, jb=jb: e.tensor_tensor(
                                    Ebuf2[:, ebase:ebase + wtot],
                                    Ebuf[:, ebase:ebase + wtot],
                                    dview(cc, lane)[:, lo0 - jb:lo0 - jb + wtot], ALU.mult),
                                    deps=[ex] + ld_D[cc] + E2_war[(lane, sl)])
                                stt_war[(lane, sl)] = [dm]
                                E_war[(lane, sl)] = [dm]

                                def mk_pv(lane=lane, j0=j0, sl=sl, lo0=lo0, ex=dm,
                                          asmop=vcp[(cc, lane)], i0=i0, nj=nj,
                                          ebase=ebase, cc=cc):
                                    def emit():
                                        outp = []
                                        for kk in range(2):
                                            blo = max(lo0, i0 + kk * 512)
                                            bhi = i0 + (kk + 1) * 512
                                            if blo >= bhi:
                                                continue
                                            lastj = min(nj - 1, (bhi - 1) // 128)
                                            pv = s.pe(lambda e, lane=lane, blo=blo, bhi=bhi,
                                                      i0=i0, sl=sl, j0=j0, lastj=lastj,
                                                      lo0=lo0, ebase=ebase: e.matmul(
                                                psU[lane][:, blo - i0:bhi - i0],
                                                vhat[:, (lane * 16 + j0) * 66:
                                                     (lane * 16 + j0) * 66 + 65],
                                                Ebuf2[:, ebase + blo - lo0:ebase + bhi - lo0],
                                                start=(j0 == 0), stop=(j0 == lastj)),
                                                deps=[ex, asmop] + (U_war[lane] if j0 == 0 else []))
                                            outp.append(pv)
                                        pv_ops[lane] += outp
                                        E2_war[(lane, sl)] = outp
                                        vh_war[lane] = vh_war[lane] + outp
                                    return emit
                                pend[lane] = mk_pv()
                        for lane in range(2):
                            if pend[lane] is not None:
                                pend[lane]()
                                pend[lane] = None
                        if ihalf == 1:
                            d_war[cc % 2] = [op for lst in (stt_war[(0, 0)], stt_war[(0, 1)],
                                                            stt_war[(1, 0)], stt_war[(1, 1)]) for op in lst]
                        ring = it_idx[0] % 2
                        it_idx[0] += 1
                        for lane in range(2):
                            ev0 = s.dv(lambda e, cc=cc, lane=lane, i0=i0: e.tensor_copy(
                                qv(cc)[lane * 64:(lane + 1) * 64, i0:i0 + 1024],
                                psU[lane][0:64, 0:1024]),
                                deps=pv_ops[lane])
                            smc = s.dv(lambda e, lane=lane, ring=ring: e.tensor_copy(
                                rsum[lane][ring], psU[lane][64:65, 0:1024]),
                                deps=pv_ops[lane] + rsum_war[(lane, ring)])
                            U_war[lane] = [ev0, smc]
                            defer[lane] = mk_defer(cc, lane, ihalf, smc, ev0, ring)
                emit_deferred(0)
                emit_deferred(1)
                gate["ps"] = (U_war[0] + U_war[1] + S_war[0] + S_war[1]
                              + rlbc_war[0] + rlbc_war[1])
            at_es.close()
            n_es.close()

            res3_dma = {}

            def rhs_out_deps(ci, tt):
                return [out_ready[(ci, lane, tt // 2)] for lane in range(2)]

            with contextlib.ExitStack() as les:
                ps_proj = les.enter_context(nc.psum_tensor(_nm("ps_wo2"), [128, 4096], F32))
                x1sc = les.enter_context(nc.sbuf_tensor(_nm("x1sc"), [128, 2 * T], F32))
                x3sc = les.enter_context(nc.sbuf_tensor(_nm("x3sc"), [128, 2 * T], F32))
                xdma = {}
                stt3 = {}
                x3prev = {}

                def ev_wo2(oc, tt, pp, mml, par):
                    slot = oc % 2
                    chan = f"str{slot}"
                    chan3 = f"xst{slot}"
                    if oc not in xdma:
                        xdma[oc] = s.dma_gp(lambda e, oc=oc, slot=slot: e.dma_start(
                            out=x1sc[:, slot * T:slot * T + T],
                            in_=d_xs1[oc * 128:(oc + 1) * 128, :]), chan,
                            deps=g_chan.get(chan, []) + gate["ps"])
                        g_chan[chan] = []
                    ev = s.dv(lambda e, oc=oc, tt=tt, pp=pp, slot=slot: e.scalar_tensor_tensor(
                        x3sc[:, slot * T + tt * 512: slot * T + (tt + 1) * 512], pp,
                        boutc[:, oc:oc + 1],
                        x1sc[:, slot * T + tt * 512: slot * T + (tt + 1) * 512],
                        ALU.add, ALU.add),
                        deps=[mml, xdma[oc], ld["boutc"], x3prev.get(slot)]
                        + ([] if slot in x3prev else gate["ps"]))
                    g_chan[chan].append(ev)
                    stt3.setdefault(oc, []).append(ev)
                    if tt == 3:
                        dmo = s.dma_gp(lambda e, oc=oc, slot=slot: e.dma_start(
                            out=d_xs2[oc * 128:(oc + 1) * 128, :],
                            in_=x3sc[:, slot * T:slot * T + T]), chan3,
                            deps=stt3[oc] + g_chan.get(chan3, []))
                        g_chan[chan3] = []
                        x3prev[slot] = dmo
                        res3_dma[oc] = dmo
                    return [ev]

                proj_fm(dw["wout"], 0, qv, rhs_out_deps, ev_wo2, ps_proj, nreg=4,
                        pre_marks=marks_w, pre_dg2=2)
            qk_es.close()

            x3_ld = {}
            for cc in range(CH):
                x3_ld[cc] = s.dma_gp(lambda e, cc=cc: e.dma_start(
                    out=arv(cc), in_=d_xs2[cc * 128:(cc + 1) * 128, :]),
                    f"xld{cc}", deps=[res3_dma[cc]] + rhs_out_deps(cc, 0)
                    + rhs_out_deps(cc, 2))

            def x3_ready(cc):
                return [x3_ld[cc]]
        else:
            x3_ready = x2_ready

        x4_ready = ffn_layer(x3_ready) if stop_after >= 4 else x3_ready

        out_dmas = []
        for cc in range(CH):
            od = s.dma_gp(lambda e, cc=cc: e.dma_start(
                out=d_out[cc * 128:(cc + 1) * 128, :], in_=arv(cc)),
                "out", deps=x4_ready(cc), bulk=True)
            out_dmas.append(od)

        s.assign(sems, mk_dma_sem)
        s.final.append(("gp", out_dmas[-1]))
        with nc.Block() as block:
            @block.tensor
            def _(eng):
                s.emit("pe", eng)

            @block.vector
            def _(eng):
                s.emit("dv", eng)

            @block.scalar
            def _(eng):
                s.emit("ac", eng)

            @block.gpsimd
            def _(eng):
                s.emit("gp", eng)

            @block.sync
            def _(eng):
                s.emit("sq", eng)

    return nc


def _host_inputs(inputs):
    f32 = np.float32
    x = np.ascontiguousarray(np.transpose(inputs["L0_attn_a_x"], (0, 2, 1)))
    ins = {}
    ins["wqT"] = np.ascontiguousarray(inputs["L0_attn_a_Wq"].T)
    ins["wvT"] = np.ascontiguousarray(inputs["L0_attn_a_Wv"].T)
    ins["wgT"] = np.ascontiguousarray(inputs["L0_attn_a_Wg"].T)
    ins["woT"] = np.ascontiguousarray(inputs["L0_attn_a_Wo"].T)
    ins["wrT"] = np.ascontiguousarray(inputs["L0_ffn_W_r"].T)
    ins["wkT"] = np.ascontiguousarray(inputs["L0_ffn_W_k"].T)
    ins["wfvT"] = np.ascontiguousarray(inputs["L0_ffn_W_v"].T)
    ins["wout"] = np.ascontiguousarray(inputs["L0_attn_b_w_out"])
    ins["wqkv"] = np.ascontiguousarray(inputs["L0_attn_b_w_qkv"])

    def cols(v):
        return np.ascontiguousarray(np.asarray(v, dtype=f32).reshape(CH, 128).T)

    ins["bout_c"] = cols(inputs["L0_attn_b_b_out"])
    ins["mur_c"] = cols(np.asarray(inputs["L0_ffn_mu_r"], dtype=f32) - 1.0)
    ins["muk_c"] = cols(np.asarray(inputs["L0_ffn_mu_k"], dtype=f32) - 1.0)
    ins["gam_c"] = cols(inputs["L0_attn_a_ln_gamma"])
    ins["bet_c"] = cols(inputs["L0_attn_a_ln_beta"])
    beta = np.asarray(inputs["L0_attn_a_beta"], dtype=f32)
    p = np.arange(128)
    ccg = np.arange(CH)
    ins["beta_pc"] = np.ascontiguousarray(beta[(2 * ccg[None, :] + (p[:, None] >= 64))])
    pp = np.arange(128)[:, None]
    cc_ = np.arange(T)[None, :]
    rel = np.where(cc_ >= pp, -(cc_ - pp).astype(np.float64), NEGBIG).astype(f32)
    ins["relmaster"] = np.ascontiguousarray(rel)
    import ml_dtypes
    slopes = _slopes16()
    Dm = np.zeros((NH * 128, T), dtype=ml_dtypes.bfloat16)
    for h in range(NH):
        Dm[h * 128:(h + 1) * 128, :] = np.exp(slopes[h] * rel.astype(np.float64)).astype(ml_dtypes.bfloat16)
    ins["Dmaster"] = Dm
    G = np.zeros((128, 32), dtype=ml_dtypes.bfloat16)
    G[0:64, 14] = 1
    G[64:128, 15] = 1
    ins["Gind"] = G
    ins["idn"] = np.eye(128, dtype=ml_dtypes.bfloat16)
    return x, ins


def kernel(**inputs):
    x, ins = _host_inputs(inputs)
    nc = build(STOP_AFTER)
    in_maps = []
    for b in range(8):
        m = dict(ins)
        m["xT"] = np.ascontiguousarray(x[b])
        in_maps.append(m)
    res = run_bass_kernel_spmd(nc, in_maps, core_ids=list(range(8)))
    out = np.stack([np.asarray(res.results[b]["out"]) for b in range(8)], axis=0)
    return np.ascontiguousarray(out.transpose(0, 2, 1)).astype(np.float32)
